# version 12
# baseline (speedup 1.0000x reference)
import contextlib
import numpy as np

import concourse.bass as bass
import concourse.mybir as mybir
from concourse.alu_op_type import AluOpType as ALU

F32 = mybir.dt.float32
BF16 = mybir.dt.bfloat16
AF = mybir.ActivationFunctionType

B, T, C = 16, 4096, 1024
NCORE = 8
BPC = B // NCORE
H, DH, HID, FF, NE = 16, 64, 256, 4096, 4
GT = 512
TPG = GT // 128
GPB = T // GT
NG_FULL = BPC * GPB
LN_EPS = 1e-5
RMS_EPS = float(np.finfo(np.float32).eps)
NSLOT = 2


class Sem:
    __slots__ = ("h", "n", "full")

    def __init__(self, h, full=False):
        self.h = h
        self.n = 0
        self.full = full

    def tok(self):
        return (self, self.n)


class KB:
    def __init__(self, nc):
        self.nc = nc
        self.ops = []
        self.counts = {k: 0 for k in ("tensor", "vector", "scalar", "gpsimd", "sync")}

    def E(self, eng, build, waits=(), inc=None, inc_by=1):
        toks = [(t[0].h, t[0].n if t[0].full else t[1]) for t in waits if t is not None]
        es_ = self.engsem.get(eng) if getattr(self, "simsafe", False) else None
        if es_ is not None and es_.n > 0:
            toks = [(es_.h, es_.n)] + toks
        ih = None
        if inc is not None:
            inc.n += inc_by
            ih = inc.h

        def fn(h, toks=toks, build=build, ih=ih, inc_by=inc_by):
            for sh, v in toks:
                h.wait_ge(sh, v)
            ins = build(h)
            if ih is not None:
                ins.then_inc(ih, inc_by)

        self.ops.append((eng, fn))
        self.counts[eng] += 1
        return inc.tok() if inc is not None else None

    def W(self, eng, waits):
        toks = [(t[0].h, t[0].n if t[0].full else t[1]) for t in waits if t is not None]

        def fn(h, toks=toks):
            for sh, v in toks:
                h.wait_ge(sh, v)

        self.ops.append((eng, fn))


def host_layout(inputs):
    f = np.ascontiguousarray

    def wT(Wx):
        o, c = Wx.shape
        t = Wx.T.reshape(c // 128, 128, o).transpose(1, 0, 2)
        return f(t.reshape(128, -1)).astype(np.float32)

    A = inputs["L0_ffn_A"]
    Bm = inputs["L0_ffn_B"]
    def wT_q(Wx):
        t = wT(Wx).reshape(128, 8, 4, 1024).transpose(0, 2, 1, 3)
        return f(t.reshape(128, -1)).astype(np.float32)

    d = {
        "wq_t": wT(inputs["L0_attn_Wq"]),
        "wk_t": wT(inputs["L0_attn_Wk"]),
        "wv_t": wT(inputs["L0_attn_Wv"]),
        "wm1_t": wT(inputs["L0_attn_Wm1"]),
        "wproj_t": wT(inputs["L0_attn_Wproj"]),
        "wfc_t": wT_q(inputs["L0_attn_Wfc"]),
        "wi_t": wT_q(inputs["L0_ffn_Wi"]),
        "wcp_t": wT(inputs["L0_attn_Wcp"]),
        "wo_t": wT(inputs["L0_ffn_Wo"]),
        "wm2_t": wT(inputs["L0_attn_Wm2"]),
    }
    d["wg_t"] = wT(inputs["L0_ffn_Wg"])
    d["a_t"] = wT(A.transpose(1, 0, 2).reshape(NE * 4, C))
    b96 = np.zeros((100, 8 * 512), dtype=np.float32)
    for e in range(NE):
        for fb in range(8):
            s = e * 8 + fb
            b0, wn = 32 * (s % 4), s // 4
            b96[b0:b0 + 4, wn * 512:(wn + 1) * 512] = Bm[e, fb * 512:(fb + 1) * 512, :].T
    d["b_t"] = f(b96)
    d["bm1c"] = f(inputs["L0_attn_bm1"].reshape(2, 128).T).astype(np.float32)
    d["bm2c"] = f(inputs["L0_attn_bm2"].reshape(16, 1)).astype(np.float32)
    d["bgc"] = f(np.tile(inputs["L0_ffn_bg"].reshape(1, 4), (128, 1))).astype(np.float32)
    return d


WIN_SHAPES = [
    ("wq_t", [128, 8 * 1024]), ("wk_t", [128, 8 * 1024]), ("wv_t", [128, 8 * 1024]),
    ("wm1_t", [128, 8 * 256]), ("wproj_t", [128, 8 * 1024]),
    ("wfc_t", [128, 8 * 4096]), ("wi_t", [128, 8 * 4096]),
    ("wcp_t", [128, 32 * 1024]), ("wo_t", [128, 32 * 1024]),
    ("wm2_t", [128, 2 * 16]), ("wg_t", [128, 8 * 4]), ("a_t", [128, 8 * 16]), ("b_t", [100, 8 * 512]),
    ("bm1c", [128, 2]), ("bm2c", [16, 1]), ("bgc", [128, 4]),
]
STREAMED = ["wq_t", "wk_t", "wv_t", "wm1_t", "wproj_t", "wfc_t", "wcp_t", "wi_t", "wo_t"]

TAP_SHAPES = {
    "ln1": [128, TPG * 1024], "qf": [128, 8 * GT], "kf": [128, 8 * GT],
    "vt": [128, TPG * 1024], "z": [16, GT], "ld": [128, 64], "pall": [128, 64],
    "yt": [128, TPG * 1024], "y2": [128, TPG * 1024], "yn": [128, TPG * 1024],
    "rsq": [128, 4 * GT], "mlp": [128, TPG * 1024],
    "xmid": [128, TPG * 1024], "w4": [128, 4], "lowt": [4, 4 * 128],
    "hwt": [128, 4 * 512], "ffn": [128, TPG * 1024],
    "sc_D": [128, 128], "sc_Am": [128, 128], "sc_Kg": [128, 64],
    "sc_kT": [128, 64], "sc_gC": [128, 1],
    "sc_Yin": [128, 64], "sc_S": [128, 256], "sc_P64": [128, 1],
}


def build_core_program(ng=NG_FULL, taps=None, stage=5, scan_limit=None):
    taps = taps or {}
    nc = bass.Bass("TRN2", target_bir_lowering=False)

    xin = nc.dram_tensor("xin", [BPC * T, C], F32, kind="ExternalInput").ap()
    xout = nc.dram_tensor("xout", [BPC * T, C], F32, kind="ExternalOutput").ap()
    win = {n: nc.dram_tensor(n, s, F32, kind="ExternalInput").ap() for n, s in WIN_SHAPES}
    wsc = {n: nc.dram_tensor("sc_" + n, list(win[n].tensor.shape), BF16, kind="Internal").ap()
           for n in STREAMED}
    tap_out = {t: nc.dram_tensor("tap_" + t, TAP_SHAPES[t], F32, kind="ExternalOutput").ap()
               for t in taps}

    kb = KB(nc)
    with contextlib.ExitStack() as es:
        def sb(name, shape, dt):
            return es.enter_context(nc.sbuf_tensor(name, shape, dt))

        def psum(name, shape, dt):
            return es.enter_context(nc.psum_tensor(name, shape, dt))

        def sem(name, full=False):
            return Sem(es.enter_context(nc.semaphore(name)), full)

        ones128b = sb("ones128b", [128, 128], BF16)
        identb = sb("identb", [128, 128], BF16)
        ident32 = sb("ident32", [128, 128], F32)
        tri32 = sb("tri32", [128, 128], F32)
        onesrow32 = sb("onesrow32", [1, 128], F32)
        wm2b = sb("wm2b", [128, 32], BF16)
        wgb = sb("wgb", [128, 8 * 4], BF16)
        atb = sb("atb", [128, 8 * 16], BF16)
        btb = sb("btb", [100, 8 * 512], BF16)
        bm1c = sb("bm1c_s", [128, 2], F32)
        bm2c = sb("bm2c_s", [16, 1], F32)
        bgc = sb("bgc_s", [128, 4], F32)
        wslot = [sb(f"wslot{i}", [128, 8192], BF16) for i in range(NSLOT)]
        xsb = [sb(f"xsb{i}", [128, TPG * 1024], F32) for i in range(2)]
        ln_tm = sb("ln_tm", [128, TPG * 1024], BF16)
        ln_fm = sb("ln_fm", [128, 8 * GT], BF16)
        q_fm = sb("q_fm", [128, 8 * GT], BF16)
        k_fm = sb("k_fm", [128, 8 * GT], BF16)
        v_tm = sb("v_tm", [128, TPG * 1024], BF16)

        z_sb = sb("z_sb", [16, GT], F32)
        z_tm = sb("z_tm", [128, 64], F32)
        u_tm = sb("u_tm", [128, 64], F32)
        t_tm = sb("t_tm", [128, 64], F32)
        w1_tm = sb("w1_tm", [128, 64], F32)
        ld_tm = sb("ld_tm", [128, 64], F32)
        e_tm = sb("e_tm", [128, 64], F32)
        g_tm = sb("g_tm", [128, 64], F32)
        P_all = sb("P_all", [128, 64], F32)
        PL_row = sb("PL_row", [1, 64], F32)
        y_tm = sb("y_tm", [128, TPG * 1024], BF16)
        y_fmb = [sb(f"y_fm{i}", [128, 8 * 128], BF16) for i in range(2)]
        m1r = y_fmb[0]
        y2_sb = sb("y2_sb", [128, TPG * 1024], BF16)
        trash = sb("trash", [128, 1024], BF16)
        ntmp = trash
        relub = trash
        bnst = [sb(f"bnst{i}", [128, 12], F32) for i in range(2)]
        mv = [sb(f"mv{i}", [128, 2], F32) for i in range(2)]
        rstd = [sb(f"rstd{i}", [128, 1], F32) for i in range(2)]
        nmr = [sb(f"nmr{i}", [128, 1], F32) for i in range(2)]
        Dsb = [sb(f"Dsb{i}", [128, 512], F32) for i in range(2)]
        Drep = [sb(f"Drep{i}", [128, 512], F32) for i in range(2)]
        Am = [sb(f"Am{i}", [128, 512], BF16) for i in range(2)]
        KgT = [sb(f"KgT{i}", [128, 64], BF16) for i in range(2)]
        Yin = [sb(f"Yin{i}", [128, 64], BF16) for i in range(2)]
        Ysc = [sb(f"Ysc{i}", [128, 64], BF16) for i in range(2)]
        gC = [sb(f"gC{i}", [128, 1], F32) for i in range(4)]
        P_bc = sb("P_bc", [128, 64], F32)
        S_sb = sb("S_sb", [128, (H // 2) * 64], BF16)
        epsc = sb("epsc", [128, 2], F32)
        lg = sb("lg", [128, 4], F32)
        gt1 = sb("gt1", [128, 12], F32)
        exs = sb("exs", [128, 4], F32)
        ssum = sb("ssum", [128, 1], F32)
        rs_g = sb("rs_g", [128, 1], F32)
        w4 = sb("w4", [128, 4 * 4], F32)
        lowT = sb("lowT", [100, TPG * 512], BF16)
        low_fm = sb("low_fm", [16, GT], BF16)
        basebuf = [sb(f"base{i}", [128, TPG * 512], BF16) for i in range(2)]
        hwbuf = basebuf
        h4 = [sb(f"h4_{i}", [128, 4 * 512], BF16) for i in range(2)]
        rqhw = sb("rqhw", [128, 32 * GT], BF16)

        pb = [psum(f"pb{i}", [128, 512], F32) for i in range(7)]
        pk = psum("pk", [128, 1024], BF16)

        SPE = sem("SPE"); SDVE = sem("SDVE"); SACT = sem("SACT"); SPOOL = sem("SPOOL")
        S_LNTR = sem("S_LNTR", full=True)
        S_YTR = sem("S_YTR", full=True)
        S_PL = sem("S_PL", full=True)
        S_LOWT = sem("S_LOWT", full=True)
        S_MLPTR = sem("S_MLPTR", full=True)
        S_HWTR = sem("S_HWTR", full=True)
        S_FFNTR = sem("S_FFNTR", full=True)
        SWSL = [sem(f"SWSL{i}", full=True) for i in range(NSLOT)]
        SX = [sem(f"SX{i}", full=True) for i in range(2)]
        SBIAS = sem("SBIAS", full=True)
        SRES = sem("SRES", full=True)
        SCAST = {n: sem("SC_" + n, full=True) for n in STREAMED}
        SWF = sem("SWF")
        SOUT = sem("SOUT", full=True)
        kb.engsem = {"vector": SDVE, "scalar": SACT, "gpsimd": SPOOL}
        import os as _os3
        kb.simsafe = bool(int(_os3.environ.get("BASS_SIMSAFE", "0")))

        E = kb.E

        NONE = None

        t_iota0 = E("gpsimd", lambda h: h.iota(Dsb[0][:, 0:128], pattern=[[0, 128]],
                                               channel_multiplier=1,
                                               allow_small_or_imprecise_dtypes=True), inc=SPOOL)
        t_iota = E("gpsimd", lambda h: h.iota(Dsb[1][:, 0:128], pattern=[[1, 128]],
                                              channel_multiplier=0,
                                              allow_small_or_imprecise_dtypes=True), inc=SPOOL)
        E("vector", lambda h: h.memset(ones128b[:], 1.0), inc=SDVE)
        E("vector", lambda h: h.memset(epsc[:, 0:1], LN_EPS), inc=SDVE)
        E("vector", lambda h: h.memset(epsc[:, 1:2], RMS_EPS), inc=SDVE)
        E("vector", lambda h: h.memset(onesrow32[:], 1.0), inc=SDVE)
        t_memset = E("vector", lambda h: h.memset(S_sb[:], 0.0), inc=SDVE)
        t_id32 = E("vector", lambda h: h.tensor_tensor(out=ident32[:], in0=Dsb[0][:, 0:128],
                                                       in1=Dsb[1][:, 0:128],
                                                       op=ALU.is_equal),
                   waits=[t_iota, t_memset], inc=SDVE)
        E("vector", lambda h: h.tensor_tensor(out=tri32[:], in0=Dsb[0][:, 0:128], in1=Dsb[1][:, 0:128],
                                              op=ALU.is_le), inc=SDVE)
        t_consts = E("vector", lambda h: h.tensor_copy(out=identb[:], in_=ident32[:]),
                     waits=[t_id32], inc=SDVE)
        E("gpsimd", lambda h: h.dma_start(out=wm2b[:], in_=win["wm2_t"]), inc=SRES, inc_by=16)
        E("gpsimd", lambda h: h.dma_start(out=wgb[:], in_=win["wg_t"]), inc=SRES, inc_by=16)
        E("gpsimd", lambda h: h.dma_start(out=atb[:], in_=win["a_t"]), inc=SRES, inc_by=16)
        t_smallw = E("gpsimd", lambda h: h.dma_start(out=btb[:], in_=win["b_t"]),
                     inc=SRES, inc_by=16)
        E("scalar", lambda h: h.dma_start(out=bm1c[:], in_=win["bm1c"]), inc=SBIAS, inc_by=16)
        E("scalar", lambda h: h.dma_start(out=bm2c[:], in_=win["bm2c"]), inc=SBIAS, inc_by=16)
        t_bias = E("scalar", lambda h: h.dma_start(out=bgc[:], in_=win["bgc"]),
                   inc=SBIAS, inc_by=16)
        cast_tok = {}
        for name in STREAMED:
            cast_tok[name] = E("gpsimd", lambda h, n=name: h.dma_start(out=wsc[n], in_=win[n]),
                               inc=SCAST[name], inc_by=16)
        x_tok = [None] * (ng + 1)
        x_tok[0] = E("scalar", lambda h: h.dma_start(
            out=xsb[0].ap().rearrange("p (j f) -> p j f", j=TPG),
            in_=xin.rearrange("(j p) f -> p j f", p=128)[:, 0:TPG, :]),
            inc=SX[0], inc_by=16)
        init_waits = [t_consts, t_memset, t_smallw, t_bias]

        group_loads = (
            [("wq_t", 0, 8192), ("wk_t", 0, 8192), ("wv_t", 0, 8192),
             ("wm1_t", 0, 2048), ("wproj_t", 0, 8192)]
            + [("wfc_t", q * 8192, 8192) for q in range(4)]
            + [("wcp_t", q * 8192, 8192) for q in range(4)]
            + [("wcp_t", q * 8192, 8192) for q in range(4)]
            + [("wi_t", q * 8192, 8192) for q in range(4)]
            + [("wo_t", q * 8192, 8192) for q in range(4)]
            + [("wo_t", q * 8192, 8192) for q in range(4)]
        )
        LPG = len(group_loads)
        load_tok = {}
        free_tok = {}

        def emit_wload(gl):
            name, ofs, ncols = group_loads[gl % LPG]
            slot = wslot[gl % NSLOT]
            waits = [cast_tok[name]]
            if gl >= NSLOT:
                waits.append(free_tok[gl - NSLOT])
            load_tok[gl] = E("gpsimd", lambda h, n=name, o=ofs, k=ncols, s=slot:
                             h.dma_start(out=s[:, 0:k], in_=wsc[n][:, o:o + k]),
                             waits=waits, inc=SWSL[gl % NSLOT], inc_by=16)

        next_load = [0]

        def pump_loads(upto):
            while next_load[0] < min(upto, ng * LPG):
                gl = next_load[0]
                if gl - NSLOT >= 0 and (gl - NSLOT) not in free_tok:
                    break
                emit_wload(gl)
                next_load[0] += 1

        pump_loads(NSLOT)

        def get_load(gl):
            pump_loads(gl + 1)
            assert gl in load_tok, f"load {gl} not emittable (missing free)"
            return load_tok[gl]

        pb_free = [None] * 7
        pk_free = NONE
        lnfm_free = None
        lntm_free = None
        x_store_tok = [None] * ng
        yfm_free = [None, None]
        h4_free = [None, None]
        base_free = [None, None]
        hw_free = [None, None]
        scan_state_tok = {}
        S_ready = t_memset
        w4_free = None
        lowT_free = None

        def tap(name, g_req, g, src_ap, wtok):
            if name in taps and taps[name] == g_req and g == g_req:
                E("gpsimd", lambda h: h.dma_start(out=tap_out[name], in_=src_ap),
                  waits=[wtok], inc=SOUT, inc_by=16)

        def ln_into(a_ap, out_ap, par, ptoks, res_x_ap=None, extra=()):
            bn = bnst[par]; m_ = mv[par]; r_ = rstd[par]; n_ = nmr[par]
            E("vector", lambda h: h.bn_stats(out=bn[:, 0:6], in_=a_ap[:, 0:512]),
              waits=[*ptoks, *extra], inc=SDVE)
            t_bn2 = E("vector", lambda h: h.bn_stats(out=bn[:, 6:12], in_=a_ap[:, 512:1024]),
                      inc=SDVE)
            t_mv = E("vector", lambda h: h.bn_aggr(out=m_[:], in_=bn[:]), waits=[t_bn2],
                     inc=SDVE)
            E("scalar", lambda h: h.activation(out=r_[:], in_=m_[:, 1:2], func=AF.Ln,
                                               bias=epsc[:, 0:1]), waits=[t_mv], inc=SACT)
            t_r = E("scalar", lambda h: h.activation(out=r_[:], in_=r_[:], func=AF.Exp,
                                                     scale=-0.5), inc=SACT)
            t_nmr = E("vector", lambda h: h.tensor_scalar(out=n_[:], in0=m_[:, 0:1],
                                                          scalar1=r_[:], scalar2=-1.0,
                                                          op0=ALU.mult, op1=ALU.mult),
                      waits=[t_r], inc=SDVE)
            t_norm = E("vector", lambda h: h.tensor_scalar(out=out_ap, in0=a_ap,
                                                           scalar1=r_[:], scalar2=n_[:],
                                                           op0=ALU.mult, op1=ALU.add),
                       waits=[t_nmr], inc=SDVE)
            t_res = None
            if res_x_ap is not None:
                t_res = E("vector", lambda h: h.tensor_tensor(out=res_x_ap, in0=res_x_ap,
                                                              in1=out_ap, op=ALU.add),
                          inc=SDVE)
            return t_norm, t_res

        LNFM3 = ln_fm.ap().rearrange("p (j f) -> p j f", j=8)
        RQ3 = rqhw.ap().rearrange("p (q f) -> p q f", q=32)

        for g in range(ng):
            base_l = g * LPG
            xcur = xsb[g % 2]
            xnxt = xsb[(g + 1) % 2]
            if g + 1 < ng:
                r0 = (g + 1) * TPG
                ws = [x_store_tok[g - 1]] if g >= 1 else []
                x_tok[g + 1] = E("scalar", lambda h, r0=r0, xb=xnxt: h.dma_start(
                    out=xb.ap().rearrange("p (j f) -> p j f", j=TPG),
                    in_=xin.rearrange("(j p) f -> p j f", p=128)[:, r0:r0 + TPG, :]),
                    waits=ws, inc=SX[(g + 1) % 2], inc_by=16)
            if g == GPB:
                wS = [scan_state_tok[hh] for hh in range(H)]
                S_ready = E("vector", lambda h: h.memset(S_sb[:], 0.0), waits=wS, inc=SDVE)
                scan_state_tok = {}

            t_ln1 = []
            for tau in range(TPG):
                t_n, _ = ln_into(xcur[:, tau * 1024:(tau + 1) * 1024],
                                 ln_tm[:, tau * 1024:(tau + 1) * 1024], tau % 2,
                                 [x_tok[g]], extra=[lntm_free, *init_waits])
                t_ln1.append(E("sync", lambda h, t=tau: h.dma_start(
                    out=LNFM3[:, :, t * 128:(t + 1) * 128],
                    in_=ln_tm[:, t * 1024:(t + 1) * 1024], transpose=True),
                    waits=[t_n, lnfm_free], inc=S_LNTR, inc_by=16))
            t_ln1_all = t_ln1[-1]
            tap("ln1", taps.get("ln1"), g, ln_tm[:], t_ln1_all)

            tq = get_load(base_l + 0)
            tk = get_load(base_l + 1)
            qk_done = {}
            for idx, (nm, wtok, dst, eng_e) in enumerate(
                    [("q", tq, q_fm, "vector"), ("k", tk, k_fm, "scalar")]):
                slot = wslot[(base_l + idx) % NSLOT]
                for och in range(8):
                    bank = och % 4
                    t_mm = None
                    for c in range(8):
                        waits = [wtok, t_ln1_all, pb_free[bank]] if c == 0 else []
                        last = (och == 7 and c == 7)
                        t_mm = E("tensor", lambda h, s=slot, cc=c, o=och, bk=bank:
                                 h.matmul(out=pb[bk][:],
                                          lhsT=s[:, cc * 1024 + o * 128:cc * 1024 + (o + 1) * 128],
                                          rhs=ln_fm[:, cc * GT:(cc + 1) * GT],
                                          start=(cc == 0), stop=(cc == 7)),
                                 waits=waits, inc=(SWF if last else SPE))
                        if last:
                            free_tok[base_l + idx] = t_mm
                    if eng_e == "vector":
                        pb_free[bank] = E("vector", lambda h, o=och, bk=bank, d=dst:
                                          h.tensor_copy(out=d[:, o * GT:(o + 1) * GT],
                                                        in_=pb[bk][:]),
                                          waits=[t_mm], inc=SDVE)
                    else:
                        pb_free[bank] = E("scalar", lambda h, o=och, bk=bank, d=dst:
                                          h.activation(out=d[:, o * GT:(o + 1) * GT],
                                                       in_=pb[bk][:], func=AF.Copy),
                                          waits=[t_mm], inc=SACT)
                qk_done[nm] = pb_free[bank]
            t_qdone = qk_done["q"]; t_kdone = qk_done["k"]
            tap("qf", taps.get("qf"), g, q_fm[:], t_qdone)
            tap("kf", taps.get("kf"), g, k_fm[:], t_kdone)

            tv = get_load(base_l + 2)
            vslot = wslot[(base_l + 2) % NSLOT]
            t_vdone = None
            for tau in range(TPG):
                bank = tau % 2
                t_mm = None
                for c in range(8):
                    w0 = [tv, t_ln1[tau], pb_free[bank], pb_free[2 + bank]] if c == 0 else []
                    last = (tau == TPG - 1 and c == 7)
                    E("tensor", lambda h, s=vslot, cc=c, t=tau, bk=bank:
                      h.matmul(out=pb[bk][:],
                               lhsT=ln_fm[:, cc * GT + t * 128:cc * GT + (t + 1) * 128],
                               rhs=s[:, cc * 1024:cc * 1024 + 512],
                               start=(cc == 0), stop=(cc == 7)), waits=w0, inc=SPE)
                    t_mm = E("tensor", lambda h, s=vslot, cc=c, t=tau, bk=bank:
                             h.matmul(out=pb[2 + bk][:],
                                      lhsT=ln_fm[:, cc * GT + t * 128:cc * GT + (t + 1) * 128],
                                      rhs=s[:, cc * 1024 + 512:cc * 1024 + 1024],
                                      start=(cc == 0), stop=(cc == 7)),
                             inc=(SWF if last else SPE))
                    if last:
                        free_tok[base_l + 2] = t_mm
                pb_free[bank] = E("vector", lambda h, t=tau, bk=bank: h.tensor_copy(
                    out=v_tm[:, t * 1024:t * 1024 + 512], in_=pb[bk][:]),
                    waits=[t_mm], inc=SDVE)
                t_vdone = E("vector", lambda h, t=tau, bk=bank: h.tensor_copy(
                    out=v_tm[:, t * 1024 + 512:(t + 1) * 1024], in_=pb[2 + bk][:]), inc=SDVE)
                pb_free[2 + bank] = t_vdone
            tap("vt", taps.get("vt"), g, v_tm[:], t_vdone)

            tm1 = get_load(base_l + 3)
            m1slot = wslot[(base_l + 3) % NSLOT]
            t_m1 = None
            for oc in range(2):
                bank = 4 + oc
                t_mm = None
                for c in range(8):
                    w0 = [tm1, t_ln1_all, pb_free[bank]] if c == 0 else []
                    last = (oc == 1 and c == 7)
                    t_mm = E("tensor", lambda h, s=m1slot, cc=c, o=oc, bk=bank:
                             h.matmul(out=pb[bk][:],
                                      lhsT=s[:, cc * 256 + o * 128:cc * 256 + (o + 1) * 128],
                                      rhs=ln_fm[:, cc * GT:(cc + 1) * GT],
                                      start=(cc == 0), stop=(cc == 7)),
                             waits=w0, inc=(SWF if last else SPE))
                    if last:
                        free_tok[base_l + 3] = t_mm
                t_m1 = E("scalar", lambda h, o=oc, bk=bank: h.activation(
                    out=m1r[:, o * GT:(o + 1) * GT], in_=pb[bk][:], func=AF.Relu,
                    bias=bm1c[:, o:o + 1]), waits=[t_mm], inc=SACT)
                pb_free[bank] = t_m1
            t_zmm = None
            for hc in range(2):
                t_zmm = E("tensor", lambda h, c=hc: h.matmul(
                    out=pb[6][0:16, 0:GT], lhsT=wm2b[:, c * 16:(c + 1) * 16],
                    rhs=m1r[:, c * GT:(c + 1) * GT], start=(c == 0), stop=(c == 1)),
                    waits=([t_m1, pb_free[6], t_smallw] if hc == 0 else []), inc=SPE)
            t_zsb = E("vector", lambda h: h.tensor_scalar(
                out=z_sb[:], in0=pb[6][0:16, 0:GT], scalar1=bm2c[:], scalar2=None,
                op0=ALU.add), waits=[t_zmm], inc=SDVE)
            pb_free[6] = t_zsb
            tap("z", taps.get("z"), g, z_sb[:], t_zsb)
            t_ztr = None
            for tau in range(TPG):
                t_ztr = E("tensor", lambda h, t=tau: h.transpose(
                    out=pb[6][:, t * 16:(t + 1) * 16],
                    in_=z_sb[0:16, t * 128:(t + 1) * 128],
                    identity=ident32[0:16, 0:16]),
                    waits=([t_zsb] if tau == 0 else []), inc=SPE)
            t_ztm = E("vector", lambda h: h.tensor_copy(out=z_tm[:], in_=pb[6][:, 0:64]),
                      waits=[t_ztr], inc=SDVE)
            pb_free[6] = t_ztm
            E("scalar", lambda h: h.activation(out=u_tm[:], in_=z_tm[:], func=AF.Abs),
              waits=[t_ztm], inc=SACT)
            E("scalar", lambda h: h.activation(out=t_tm[:], in_=u_tm[:], func=AF.Exp,
                                               scale=-1.0), inc=SACT)
            E("scalar", lambda h: h.activation(out=w1_tm[:], in_=t_tm[:], func=AF.Ln,
                                               bias=1.0), inc=SACT)
            t_sp = E("scalar", lambda h: h.activation(out=u_tm[:], in_=z_tm[:],
                                                      func=AF.Relu), inc=SACT)
            t_ld = E("vector", lambda h: h.scalar_tensor_tensor(
                out=ld_tm[:], in0=u_tm[:], scalar=-1.0, in1=w1_tm[:],
                op0=ALU.mult, op1=ALU.subtract), waits=[t_sp], inc=SDVE)
            t_e = E("scalar", lambda h: h.activation(out=e_tm[:], in_=ld_tm[:],
                                                     func=AF.Exp), waits=[t_ld], inc=SACT)
            t_g = E("vector", lambda h: h.tensor_scalar(
                out=g_tm[:], in0=e_tm[:], scalar1=-1.0, scalar2=1.0,
                op0=ALU.mult, op1=ALU.add), waits=[t_e], inc=SDVE)
            tap("ld", taps.get("ld"), g, ld_tm[:], t_ld)
            t_lam = E("tensor", lambda h: h.matmul(
                out=pb[6][:, 64:128], lhsT=tri32[:], rhs=ld_tm[:],
                start=True, stop=True), waits=[t_ld, pb_free[6]], inc=SPE)
            t_P = E("scalar", lambda h: h.activation(out=P_all[:], in_=pb[6][:, 64:128],
                                                     func=AF.Exp), waits=[t_lam], inc=SACT)
            pb_free[6] = t_P
            t_PL = E("sync", lambda h: h.dma_start(out=PL_row[:], in_=P_all[127:128, :]),
                     waits=[t_P], inc=S_PL, inc_by=16)
            t_pmm = E("tensor", lambda h: h.matmul(
                out=pb[6][:, 0:64], lhsT=onesrow32[0:1, 0:128], rhs=PL_row[0:1, 0:64],
                start=True, stop=True), waits=[t_PL, pb_free[6]], inc=SPE)
            t_pbc = E("vector", lambda h: h.tensor_copy(out=P_bc[:], in_=pb[6][:, 0:64]),
                      waits=[t_pmm], inc=SDVE)
            pb_free[6] = t_pbc
            tap("pall", taps.get("pall"), g, P_all[:], t_P)
            lnfm_free = free_tok[base_l + 3]
            lntm_free = t_ln1_all

            if stage < 2:
                continue
            _ho = list(range(0, H, 2)) + list(range(1, H, 2))
            chead = [(tau, hh) for tau in range(TPG) for hh in _ho]
            NCH = len(chead)
            NB = NCH // 4
            tok_batch_pe = [None] * NB
            tok_batch_dve = [None] * NB
            tok_kgt = [None] * NCH
            tok_yin = [None] * NCH
            tok_kt = [None] * NCH
            tok_yT = [None] * NCH
            tok_sup = [None] * NCH
            tokD = [None] * NCH

            def q_sl(tau, hh):
                oc = hh >> 1; bp = 64 * (hh & 1)
                return q_fm[bp:bp + 64, oc * GT + tau * 128:oc * GT + (tau + 1) * 128]

            def k_sl(tau, hh):
                oc = hh >> 1; bp = 64 * (hh & 1)
                return k_fm[bp:bp + 64, oc * GT + tau * 128:oc * GT + (tau + 1) * 128]

            def v_sl(tau, hh):
                return v_tm[:, tau * 1024 + hh * 64:tau * 1024 + (hh + 1) * 64]

            def S_sl(hh):
                bp = 64 * (hh & 1)
                return S_sb[bp:bp + 64, (hh >> 1) * 64:((hh >> 1) + 1) * 64]

            NCH_RUN = NCH if scan_limit is None else min(scan_limit, NCH)
            NB_RUN = max(1, NCH_RUN // 4)

            def emit_batchA(b):
                par = b % 2
                bD = 3 if par == 0 else 0
                bA = 2 if par == 0 else 1
                tp = None
                for u in range(4):
                    tau, hh = chead[4 * b + u]
                    cl = tau * 16 + hh
                    wA = [t_g]
                    if u == 0 and b >= 2:
                        wA.append(tok_batch_pe[b - 2])
                    tp = E("gpsimd", lambda h, p=par, c=cl, u_=u: h.tensor_scalar(
                        out=Drep[p][:, u_ * 128:(u_ + 1) * 128], in0=ones128b[:],
                        scalar1=e_tm[:, c:c + 1], scalar2=1.0,
                        op0=ALU.mult, op1=ALU.mult), waits=wA, inc=SPOOL)
                import os as _os2
                _sp2 = int(_os2.environ.get("SCANPART", "3"))
                tt = None
                for u in range(4):
                    w = [tp, t_qdone, t_kdone] if u == 0 else []
                    if u == 0 and b >= 2:
                        w.append(tok_batch_dve[b - 2])
                    if u == 0 and b < 2:
                        w += [pb_free[bD], pb_free[bA]]
                    if _sp2 == 11:
                        tt = E("tensor", lambda h: h.matmul(
                            out=pb[6][0:1, 0:1], lhsT=onesrow32[0:1, 0:1],
                            rhs=onesrow32[0:1, 0:1], start=True, stop=True),
                            waits=w, inc=SPE)
                        continue
                    tt = E("tensor", lambda h, p=par, bd=bD, u_=u: h.transpose(
                        out=pb[bd][:, u_ * 128:(u_ + 1) * 128],
                        in_=Drep[p][:, u_ * 128:(u_ + 1) * 128],
                        identity=ident32[:]), waits=w, inc=SPE)
                for u in range(4):
                    tau, hh = chead[4 * b + u]
                    if _sp2 == 10:
                        continue
                    uu = 0 if _sp2 == 13 else u
                    if _sp2 == 15:
                        hh = 0
                    if _sp2 == 17:
                        hh = 1
                    if _sp2 == 16:
                        tt = E("tensor", lambda h, t=tau, h_=hh, u_=u: h.matmul(
                            out=pb[2][:, u_ * 128:(u_ + 1) * 128], lhsT=k_sl(t, h_),
                            rhs=q_sl(t, h_), start=True, stop=True), inc=SPE)
                        continue
                    tt = E("tensor", lambda h, t=tau, h_=hh, ba=bA, u_=uu: h.matmul(
                        out=pb[ba][:, u_ * 128:(u_ + 1) * 128], lhsT=k_sl(t, h_),
                        rhs=q_sl(t, h_), start=True, stop=True), inc=SPE)
                tok_batch_pe[b] = tt

            def emit_batchB(b):
                par = b % 2
                bD = 3 if par == 0 else 0
                bA = 2 if par == 0 else 1
                td = None
                for u in range(4):
                    w = [tok_batch_pe[b]] if u == 0 else []
                    td = E("vector", lambda h, p=par, bd=bD, u_=u: h.tensor_tensor_scan(
                        out=Dsb[p][:, u_ * 128:(u_ + 1) * 128],
                        data0=pb[bd][:, u_ * 128:(u_ + 1) * 128], data1=identb[:],
                        initial=0.0, op0=ALU.mult, op1=ALU.add), waits=w, inc=SDVE)
                ta = None
                for u in range(4):
                    tau, hh = chead[4 * b + u]
                    cl = tau * 16 + hh
                    w = []
                    if u == 0 and b >= 1:
                        if b >= 2:
                            w.append(tok_sup[4 * (b - 2) + 3])
                    ta = E("vector", lambda h, p=par, ba=bA, c=cl, u_=u:
                           h.scalar_tensor_tensor(
                        out=Am[p][:, u_ * 128:(u_ + 1) * 128],
                        in0=pb[ba][:, u_ * 128:(u_ + 1) * 128],
                        scalar=g_tm[:, c:c + 1], in1=Dsb[p][:, u_ * 128:(u_ + 1) * 128],
                        op0=ALU.mult, op1=ALU.mult), waits=w, inc=SDVE)
                for u in range(4):
                    tau, hh = chead[4 * b + u]
                    cl = tau * 16 + hh
                    E("vector", lambda h, p=par, c=cl, u_=u, g2=(4 * b + u) % 4:
                      h.tensor_tensor(
                        out=gC[g2][:], in0=g_tm[:, c:c + 1],
                        in1=Dsb[p][:, u_ * 128 + 127:u_ * 128 + 128], op=ALU.mult),
                      waits=([tok_kgt[4 * b + u - 4]] if 4 * b + u >= 4 else []),
                      inc=SDVE)
                tok_batch_dve[b] = SDVE.tok()

            def emit_headC(j):
                tau, hh = chead[j]
                if j == 0:
                    t0_, h0_ = chead[0]
                    b0_ = 64 * (h0_ & 1)
                    tok_kt[0] = E("tensor", lambda h, t=t0_, h_=h0_, bp=b0_: h.transpose(
                        out=pk[:, 0:64], in_=k_sl(t, h_),
                        identity=identb[bp:bp + 64, bp:bp + 64]),
                        waits=[pk_free], inc=SPE)
                par = j % 2
                b = j // 4
                u = j % 4
                bpar = b % 2
                bA = 2 if bpar == 0 else 1
                cl = tau * 16 + hh
                bp_h = 64 * (hh & 1)
                wkg = [tok_batch_dve[b], tok_kt[j]]
                if j >= 2:
                    wkg.append(tok_sup[j - 2])
                tok_kgt[j] = E("scalar", lambda h, p=par, g2=j % 4: h.activation(
                    out=KgT[p][:], in_=pk[:, p * 512:p * 512 + 64], func=AF.Copy,
                    scale=gC[g2][:]), waits=wkg, inc=SACT)
                bky = 4 + par
                wc = [tok_batch_dve[b], t_vdone, S_ready, t_PL]
                if hh in scan_state_tok:
                    wc.append(scan_state_tok[hh])
                if j >= 2:
                    wc += [tok_yT[j - 2]]
                else:
                    wc += [pb_free[bky]]
                t_yintra = E("tensor", lambda h, p=bpar, ba=bA, u_=u, t=tau, h_=hh,
                             bk=bky: h.matmul(
                    out=pb[bk][:, 0:64], lhsT=Am[p][:, u_ * 128:(u_ + 1) * 128],
                    rhs=v_sl(t, h_), start=True, stop=True), waits=wc, inc=SPE)
                t_yinter = E("tensor", lambda h, t=tau, h_=hh, bk=bky: h.matmul(
                    out=pb[bk][:, 64:128], lhsT=q_sl(t, h_),
                    rhs=S_sl(h_), start=True, stop=True), inc=SPE)
                if j + 1 < NCH_RUN:
                    tn, hn = chead[j + 1]
                    bpn = 64 * (hn & 1)
                    tok_kt[j + 1] = E("tensor", lambda h, t=tn, h_=hn, p=(j + 1) % 2,
                                      bp=bpn: h.transpose(
                        out=pk[:, p * 512:p * 512 + 64], in_=k_sl(t, h_),
                        identity=identb[bp:bp + 64, bp:bp + 64]),
                        waits=[tok_kgt[j]], inc=SPE)
                wsup = [tok_kgt[j]]
                wsup.append(tokD[j - 1] if j >= 1 else pb_free[6])
                tok_sup[j] = E("tensor", lambda h, p=par, t=tau, h_=hh, bp=bp_h: h.matmul(
                    out=pb[6][bp:bp + 64, 0:64], lhsT=KgT[p][:], rhs=v_sl(t, h_),
                    start=True, stop=True, tile_position=(0, bp)), waits=wsup, inc=SPE)
                tokC_pe = tok_sup[j]
                wyi = [t_yintra] + ([tok_yT[j - 2]] if j >= 2 else [])
                tok_yin[j] = E("scalar", lambda h, p=par, bk=bky: h.activation(
                    out=Yin[p][:], in_=pb[bk][:, 0:64], func=AF.Copy), waits=wyi, inc=SACT)
                tok_yT[j] = E("vector", lambda h, t=tau, h_=hh, c=cl, p=par, bk=bky:
                              h.scalar_tensor_tensor(
                    out=y_tm[:, t * 1024 + h_ * 64:t * 1024 + (h_ + 1) * 64],
                    in0=pb[bk][:, 64:128], scalar=P_all[:, c:c + 1],
                    in1=Yin[p][:], op0=ALU.mult, op1=ALU.add),
                    waits=[t_yinter, tok_yin[j]], inc=SDVE)
                tokD[j] = E("vector", lambda h, c=cl, h_=hh, bp=bp_h:
                            h.scalar_tensor_tensor(
                    out=S_sl(h_), in0=S_sl(h_), scalar=P_bc[bp:bp + 64, c:c + 1],
                    in1=pb[6][bp:bp + 64, 0:64], op0=ALU.mult, op1=ALU.add),
                    waits=[tok_sup[j]], inc=SDVE)
                scan_state_tok[hh] = tokD[j]

            for s in range(NB_RUN + 2):
                if s < NB_RUN:
                    emit_batchA(s)
                if s >= 2:
                    for u in range(4):
                        j = 4 * (s - 2) + u
                        if j < NCH_RUN:
                            emit_headC(j)
                if 1 <= s <= NB_RUN:
                    emit_batchB(s - 1)
            NL = NCH_RUN - 1
            t_scan = tokD[NL]
            pb_free[4 + (NL % 2)] = tok_yT[NL]
            pb_free[4 + ((NL - 1) % 2)] = tok_yT[NL - 1] if NL >= 1 else tok_yT[NL]
            pb_free[6] = tokD[NL]
            for bb in (0, 1, 2, 3):
                pb_free[bb] = tok_batch_dve[NB_RUN - 1]
            pk_free = tok_kgt[NL]
            tap("yt", taps.get("yt"), g, y_tm[:], tok_yT[NL])

            if stage < 3:
                continue
            tproj = get_load(base_l + 4)
            projslot = wslot[(base_l + 4) % NSLOT]
            t_yn_tr = []
            for tau in range(TPG):
                par = tau % 2
                bk0 = (2 * tau) % 4
                bk1 = (2 * tau + 1) % 4
                ty_tr = E("sync", lambda h, t=tau, p=par: h.dma_start(
                    out=y_fmb[p].ap().rearrange("p (j f) -> p j f", j=8),
                    in_=y_tm[:, t * 1024:(t + 1) * 1024], transpose=True),
                    waits=[tok_yT[tau * H + H - 1], yfm_free[par]], inc=S_YTR, inc_by=16)
                t_mm = None
                for c in range(8):
                    w0 = [ty_tr, tproj, pb_free[bk0], pb_free[bk1]] if c == 0 else []
                    last = (tau == TPG - 1 and c == 7)
                    E("tensor", lambda h, s=projslot, cc=c, p=par, b=bk0: h.matmul(
                        out=pb[b][:], lhsT=y_fmb[p][:, cc * 128:(cc + 1) * 128],
                        rhs=s[:, cc * 1024:cc * 1024 + 512],
                        start=(cc == 0), stop=(cc == 7)), waits=w0, inc=SPE)
                    t_mm = E("tensor", lambda h, s=projslot, cc=c, p=par, b=bk1: h.matmul(
                        out=pb[b][:], lhsT=y_fmb[p][:, cc * 128:(cc + 1) * 128],
                        rhs=s[:, cc * 1024 + 512:cc * 1024 + 1024],
                        start=(cc == 0), stop=(cc == 7)), inc=(SWF if last else SPE))
                    if last:
                        free_tok[base_l + 4] = t_mm
                yfm_free[par] = t_mm
                pb_free[bk0] = E("vector", lambda h, t=tau, b=bk0: h.tensor_copy(
                    out=y2_sb[:, t * 1024:t * 1024 + 512], in_=pb[b][:]),
                    waits=[t_mm], inc=SDVE)
                t_y2e = E("vector", lambda h, t=tau, b=bk1: h.tensor_copy(
                    out=y2_sb[:, t * 1024 + 512:(t + 1) * 1024], in_=pb[b][:]), inc=SDVE)
                pb_free[bk1] = t_y2e
                t_sqacc = E("scalar", lambda h, t=tau, p=par: h.activation(
                    out=trash[:], in_=y2_sb[:, t * 1024:(t + 1) * 1024], func=AF.Square,
                    accum_out=rstd[p][:]), waits=[t_y2e], inc=SACT)
                E("scalar", lambda h, p=par: h.activation(
                    out=rstd[p][:], in_=rstd[p][:], func=AF.Ln, scale=1.0 / 1024.0,
                    bias=epsc[:, 1:2]), waits=[t_sqacc], inc=SACT)
                t_rs = E("scalar", lambda h, p=par: h.activation(
                    out=rstd[p][:], in_=rstd[p][:], func=AF.Exp, scale=-0.5), inc=SACT)
                t_yn = E("vector", lambda h, t=tau, p=par: h.tensor_scalar(
                    out=ln_tm[:, t * 1024:(t + 1) * 1024],
                    in0=y2_sb[:, t * 1024:(t + 1) * 1024], scalar1=rstd[p][:],
                    scalar2=None, op0=ALU.mult), waits=[t_rs, lntm_free], inc=SDVE)
                t_yn_tr.append(E("sync", lambda h, t=tau: h.dma_start(
                    out=LNFM3[:, :, t * 128:(t + 1) * 128],
                    in_=ln_tm[:, t * 1024:(t + 1) * 1024], transpose=True),
                    waits=[t_yn, lnfm_free], inc=S_LNTR, inc_by=16))
            t_yn_all = t_yn_tr[-1]
            tap("y2", taps.get("y2"), g, y2_sb[:], pb_free[1])
            tap("yn", taps.get("yn"), g, ln_tm[:], SDVE.tok())
            lntm_free = t_yn_all

            if stage < 4:
                continue
            t_sq = [None, None]
            t_sq_all = [None] * 32
            t_fc_last = None
            for Fc in range(32):
                qtr = Fc // 8
                slot = wslot[(base_l + 5 + qtr) % NSLOT]
                wt = get_load(base_l + 5 + qtr)
                bank = Fc % 4
                t_mm = None
                for c in range(8):
                    w0 = [wt, t_yn_all, pb_free[bank]] if c == 0 else []
                    last = (Fc % 8 == 7 and c == 7)
                    t_mm = E("tensor", lambda h, s=slot, cc=c, f=Fc, bk=bank: h.matmul(
                        out=pb[bk][:],
                        lhsT=s[:, cc * 1024 + (f % 8) * 128:cc * 1024 + (f % 8 + 1) * 128],
                        rhs=ln_fm[:, cc * GT:(cc + 1) * GT],
                        start=(cc == 0), stop=(cc == 7)),
                        waits=w0, inc=(SWF if last else SPE))
                    if last:
                        free_tok[base_l + 5 + qtr] = t_mm
                wr = [t_mm] + ([t_sq[Fc % 2]] if t_sq[Fc % 2] is not None else [])
                t_rel = E("vector", lambda h, bk=bank, f=Fc: h.tensor_scalar(
                    out=relub[:, (f % 2) * 512:(f % 2 + 1) * 512], in0=pb[bk][:],
                    scalar1=0.0, scalar2=None, op0=ALU.max), waits=wr, inc=SDVE)
                pb_free[bank] = t_rel
                t_sq[Fc % 2] = E("scalar", lambda h, f=Fc: h.activation(
                    out=rqhw[:, f * GT:(f + 1) * GT],
                    in_=relub[:, (f % 2) * 512:(f % 2 + 1) * 512], func=AF.Square),
                    waits=[t_rel], inc=SACT)
                t_sq_all[Fc] = t_sq[Fc % 2]
            t_rsq0, t_rsq1 = t_sq[0], t_sq[1]
            lnfm_free = free_tok[base_l + 8]
            tap("rsq", taps.get("rsq"), g, rqhw[:, 0:4 * GT], t_rsq1)

            t_xmid = [None] * TPG
            for half in range(2):
                t_cp_ev = [[None, None], [None, None]]
                for qtr in range(4):
                    gl = base_l + 9 + half * 4 + qtr
                    wt = get_load(gl)
                    slot = wslot[gl % NSLOT]
                    for u in range(4):
                        tau = half * 2 + (u >> 1)
                        ch = u & 1
                        bank = u
                        t_mm = None
                        for f8 in range(8):
                            Fc = qtr * 8 + f8
                            w0 = []
                            if f8 == 0:
                                w0 = [wt, t_sq_all[qtr * 8 + 7]]
                                if qtr == 0:
                                    w0 += [pb_free[bank]]
                            last_q = (u == 3 and f8 == 7)
                            t_mm = E("tensor", lambda h, s=slot, f=Fc, t=tau, c2=ch,
                                     bk=bank, f8_=f8: h.matmul(
                                out=pb[bk][:],
                                lhsT=rqhw[:, f * GT + t * 128:f * GT + (t + 1) * 128],
                                rhs=s[:, f8_ * 1024 + c2 * 512:f8_ * 1024 + (c2 + 1) * 512],
                                start=(f == 0), stop=(f == 31)),
                                waits=w0, inc=(SWF if last_q else SPE))
                            if last_q:
                                free_tok[gl] = t_mm
                        if qtr == 3:
                            t_ev = E("vector", lambda h, t=tau, c2=ch, bk=bank:
                                     h.tensor_tensor(
                                out=ln_tm[:, t * 1024 + c2 * 512:t * 1024 + (c2 + 1) * 512],
                                in0=pb[bk][:],
                                in1=y2_sb[:, t * 1024 + c2 * 512:t * 1024 + (c2 + 1) * 512],
                                op=ALU.add), waits=[t_mm, lntm_free], inc=SDVE)
                            pb_free[bank] = t_ev
                            t_cp_ev[u >> 1][ch] = t_ev
                for ut in range(2):
                    tau = half * 2 + ut
                    t_n, t_r = ln_into(ln_tm[:, tau * 1024:(tau + 1) * 1024], ntmp[:],
                                       tau % 2, [t_cp_ev[ut][0], t_cp_ev[ut][1]],
                                       res_x_ap=xcur[:, tau * 1024:(tau + 1) * 1024])
                    t_xmid[tau] = t_r
            lntm_free = t_xmid[-1]
            tap("xmid", taps.get("xmid"), g, xcur[:], t_xmid[-1])

            if stage < 5:
                continue
            t_ln2 = []
            for tau in range(TPG):
                t_n, _ = ln_into(xcur[:, tau * 1024:(tau + 1) * 1024],
                                 ln_tm[:, tau * 1024:(tau + 1) * 1024], tau % 2,
                                 [t_xmid[tau]], extra=[lntm_free])
                t_ln2.append(E("sync", lambda h, t=tau: h.dma_start(
                    out=LNFM3[:, :, t * 128:(t + 1) * 128],
                    in_=ln_tm[:, t * 1024:(t + 1) * 1024], transpose=True),
                    waits=[t_n, lnfm_free], inc=S_LNTR, inc_by=16))
            t_ln2_all = t_ln2[-1]
            lntm_free = t_ln2_all

            t_w4 = [None] * TPG
            for tau in range(TPG):
                t_mm = None
                for c in range(8):
                    w0 = [t_ln2[tau], pb_free[6]] if c == 0 else []
                    t_mm = E("tensor", lambda h, cc=c, t=tau: h.matmul(
                        out=pb[6][:, 0:4],
                        lhsT=ln_fm[:, cc * GT + t * 128:cc * GT + (t + 1) * 128],
                        rhs=wgb[:, cc * 4:(cc + 1) * 4],
                        start=(cc == 0), stop=(cc == 7)), waits=w0, inc=SPE)
                wlg = [t_mm] + ([w4_free] if w4_free is not None else [])
                t0 = E("vector", lambda h: h.tensor_copy(out=lg[:], in_=pb[6][:, 0:4]),
                       waits=wlg, inc=SDVE)
                pb_free[6] = t0
                t0 = E("vector", lambda h: h.tensor_tensor(out=lg[:], in0=lg[:], in1=bgc[:],
                                                           op=ALU.add),
                       waits=[t_bias, t0], inc=SDVE)
                t0 = E("vector", lambda h: h.tensor_tensor(out=gt1[:, 0:2], in0=lg[:, 0:4:2],
                                                           in1=lg[:, 1:4:2], op=ALU.max),
                       waits=[t0], inc=SDVE)
                t0 = E("vector", lambda h: h.tensor_tensor(out=gt1[:, 2:4], in0=lg[:, 0:4:2],
                                                           in1=lg[:, 1:4:2], op=ALU.min),
                       waits=[t0], inc=SDVE)
                t0 = E("vector", lambda h: h.tensor_tensor(out=gt1[:, 4:5], in0=gt1[:, 0:1],
                                                           in1=gt1[:, 1:2], op=ALU.max),
                       waits=[t0], inc=SDVE)
                t0 = E("vector", lambda h: h.tensor_tensor(out=gt1[:, 5:6], in0=gt1[:, 0:1],
                                                           in1=gt1[:, 1:2], op=ALU.min),
                       waits=[t0], inc=SDVE)
                t0 = E("vector", lambda h: h.tensor_tensor(out=gt1[:, 6:7], in0=gt1[:, 2:3],
                                                           in1=gt1[:, 3:4], op=ALU.max),
                       waits=[t0], inc=SDVE)
                t0 = E("vector", lambda h: h.tensor_tensor(out=gt1[:, 7:8], in0=gt1[:, 5:6],
                                                           in1=gt1[:, 6:7], op=ALU.max),
                       waits=[t0], inc=SDVE)
                t_gt = E("vector", lambda h: h.tensor_scalar(
                    out=gt1[:, 8:9], in0=gt1[:, 4:5], scalar1=-1.0, scalar2=None,
                    op0=ALU.mult), waits=[t0], inc=SDVE)
                t_ex = E("scalar", lambda h: h.activation(
                    out=exs[:], in_=lg[:], func=AF.Exp, bias=gt1[:, 8:9],
                    accum_out=ssum[:]), waits=[t_gt], inc=SACT)
                t0 = E("vector", lambda h: h.reciprocal(out=rs_g[:], in_=ssum[:]),
                       waits=[t_ex], inc=SDVE)
                t0 = E("vector", lambda h: h.tensor_scalar(out=exs[:], in0=exs[:],
                                                           scalar1=rs_g[:], scalar2=None,
                                                           op0=ALU.mult),
                       waits=[t0], inc=SDVE)
                t0 = E("vector", lambda h, t=tau: h.tensor_scalar(
                    out=w4[:, t * 4:(t + 1) * 4], in0=lg[:], scalar1=gt1[:, 7:8],
                    scalar2=None, op0=ALU.is_ge), waits=[t0], inc=SDVE)
                t_w4[tau] = E("vector", lambda h, t=tau: h.tensor_tensor(
                    out=w4[:, t * 4:(t + 1) * 4], in0=w4[:, t * 4:(t + 1) * 4],
                    in1=exs[:], op=ALU.mult), waits=[t0], inc=SDVE)
            w4_free = t_w4[-1]
            tap("w4", taps.get("w4"), g, w4[:, 0:4], t_w4[0])

            t_mm = None
            for c in range(8):
                w0 = [t_ln2_all, pb_free[5], t_smallw] if c == 0 else []
                t_mm = E("tensor", lambda h, cc=c: h.matmul(
                    out=pb[5][0:16, 0:GT], lhsT=atb[:, cc * 16:(cc + 1) * 16],
                    rhs=ln_fm[:, cc * GT:(cc + 1) * GT],
                    start=(cc == 0), stop=(cc == 7)), waits=w0, inc=SPE)
            t_lowe = E("vector", lambda h: h.tensor_copy(out=low_fm[:], in_=pb[5][0:16, 0:GT]),
                       waits=[t_mm], inc=SDVE)
            pb_free[5] = t_lowe
            for b0 in (0, 32, 64, 96):
                for t4 in range(TPG):
                    E("sync", lambda h, b=b0, t=t4: h.dma_start(
                        out=lowT.ap()[b:b + 4, t * 512:(t + 1) * 512].rearrange(
                            "r (e f) -> r e f", e=4),
                        in_=low_fm[0:16, t * 128:(t + 1) * 128]),
                        waits=([t_lowe] if (b0 == 0 and t4 == 0) else []),
                        inc=S_LOWT, inc_by=16)
            t_lowT_all = S_LOWT.tok()
            tap("lowt", taps.get("lowt"), g, lowT[0:4, 0:512], t_lowT_all)

            wislots = [(base_l + 17 + q) for q in range(4)]
            t_hw_tr = [[None] * TPG for _ in range(8)]
            t_combine_last = None
            t_lo_last = None
            for fb in range(8):
                qtr = fb // 2
                slot = wslot[wislots[qtr] % NSLOT]
                wt = get_load(wislots[qtr])
                bpar = fb % 2
                t_bes = [None] * TPG
                for tau in range(TPG):
                    bank = tau % 3
                    t_mm = None
                    for c in range(8):
                        w0 = []
                        if c == 0:
                            w0 = [wt, t_ln2[tau], pb_free[bank]]
                            if tau == 0:
                                w0.append(base_free[bpar])
                        last = (fb % 2 == 1 and tau == TPG - 1 and c == 7)
                        t_mm = E("tensor", lambda h, s=slot, cc=c, f=fb, t=tau, bk=bank:
                                 h.matmul(out=pb[bk][:],
                                          lhsT=ln_fm[:, cc * GT + t * 128:cc * GT + (t + 1) * 128],
                                          rhs=s[:, cc * 1024 + (f % 2) * 512:cc * 1024 + (f % 2) * 512 + 512],
                                          start=(cc == 0), stop=(cc == 7)),
                                 waits=w0, inc=(SWF if last else SPE))
                        if last:
                            free_tok[wislots[qtr]] = t_mm
                    wbe = [t_mm] + ([hw_free[bpar]] if (tau == 0 and hw_free[bpar]
                                                        is not None) else [])
                    t_bes[tau] = E("scalar", lambda h, t=tau, bk=bank, p=bpar:
                                   h.activation(
                        out=basebuf[p][:, t * 512:(t + 1) * 512], in_=pb[bk][:],
                        func=AF.Copy), waits=wbe, inc=SACT)
                    pb_free[bank] = t_bes[tau]
                for tau in range(TPG):
                    hpar = (fb * TPG + tau) % 2
                    t_rw = [None] * 4
                    t_id = [None] * 4
                    for ep in range(2):
                        for e_ in (2 * ep, 2 * ep + 1):
                            bke = 3 + e_
                            wl = [t_lowT_all, t_bes[tau]] if e_ == 0 else []
                            wl.append(pb_free[bke])
                            s_idx = e_ * 8 + fb
                            b0s = 32 * (s_idx % 4)
                            wns = s_idx // 4
                            t_lo_last = E("tensor", lambda h, ee=e_, t=tau, bk=bke,
                                          b0=b0s, wn=wns: h.matmul(
                                out=pb[bk][:],
                                lhsT=lowT[b0:b0 + 4, t * 512 + ee * 128:t * 512 + (ee + 1) * 128],
                                rhs=btb[b0:b0 + 4, wn * 512:(wn + 1) * 512],
                                start=True, stop=False, tile_position=(b0, 0)),
                              waits=wl, inc=SPE)
                        for e_ in (2 * ep, 2 * ep + 1):
                            bke = 3 + e_
                            t_id[e_] = E("tensor", lambda h, f=fb, t=tau, bk=bke,
                                         p=bpar: h.matmul(
                                out=pb[bk][:], lhsT=identb[:],
                                rhs=basebuf[p][:, t * 512:(t + 1) * 512],
                                start=False, stop=True), inc=SPE)
                        for e_ in (2 * ep, 2 * ep + 1):
                            bke = 3 + e_
                            wre = [t_id[e_], t_w4[tau], h4_free[hpar]]
                            if e_ % 2 == 0:
                                t_rw[e_] = E("scalar", lambda h, ee=e_, t=tau, bk=bke,
                                             p=hpar: h.activation(
                                    out=h4[p][:, ee * 512:(ee + 1) * 512],
                                    in_=pb[bk][:], func=AF.Relu,
                                    scale=w4[:, t * 4 + ee:t * 4 + ee + 1]),
                                    waits=wre, inc=SACT)
                            else:
                                t_rw[e_] = E("vector", lambda h, ee=e_, t=tau, bk=bke,
                                             p=hpar: h.tensor_scalar(
                                    out=h4[p][:, ee * 512:(ee + 1) * 512],
                                    in0=pb[bk][:], scalar1=0.0,
                                    scalar2=w4[:, t * 4 + ee:t * 4 + ee + 1],
                                    op0=ALU.max, op1=ALU.mult),
                                    waits=wre, inc=SDVE)
                            pb_free[bke] = t_rw[e_]
                    t_s01 = E("vector", lambda h, p=hpar: h.tensor_tensor(
                        out=h4[p][:, 0:512], in0=h4[p][:, 0:512], in1=h4[p][:, 512:1024],
                        op=ALU.add), waits=[t_rw[0], t_rw[1]], inc=SDVE)
                    t_s23 = E("gpsimd", lambda h, p=hpar: h.tensor_tensor(
                        out=h4[p][:, 1024:1536], in0=h4[p][:, 1024:1536],
                        in1=h4[p][:, 1536:2048], op=ALU.add),
                        waits=[t_rw[2], t_rw[3]], inc=SPOOL)
                    whw = [t_s01, t_s23]
                    if hw_free[bpar] is not None and tau == 0:
                        whw.append(hw_free[bpar])
                    t_hw = E("vector", lambda h, p=hpar, q=bpar, t=tau: h.tensor_tensor(
                        out=hwbuf[q][:, t * 512:(t + 1) * 512], in0=h4[p][:, 0:512],
                        in1=h4[p][:, 1024:1536], op=ALU.add), waits=whw, inc=SDVE)
                    h4_free[hpar] = t_hw
                    t_combine_last = t_hw
                    wtr = [t_hw, free_tok[base_l + 16]]
                    t_hw_tr[fb][tau] = E("sync", lambda h, f=fb, t=tau, q=bpar: h.dma_start(
                        out=RQ3[:, 4 * f:4 * (f + 1), t * 128:(t + 1) * 128],
                        in_=hwbuf[q][:, t * 512:(t + 1) * 512], transpose=True),
                        waits=wtr, inc=S_HWTR, inc_by=16)
                base_free[bpar] = t_id[3]
                hw_free[bpar] = t_hw_tr[fb][TPG - 1]
            w4_free = t_combine_last
            lowT_free = t_lo_last
            tap("hwt", taps.get("hwt"), g, hwbuf[1][:, 0:2048], t_combine_last)

            ffn_tm = v_tm
            woslots = [(base_l + 21 + q) for q in range(4)]
            t_fin = [None] * TPG
            for half in range(2):
                t_wo_ev = [[None, None], [None, None]]
                for qtr in range(4):
                    gl = base_l + 21 + half * 4 + qtr
                    wt = get_load(gl)
                    slot = wslot[gl % NSLOT]
                    for u in range(4):
                        tau = half * 2 + (u >> 1)
                        ch = u & 1
                        bank = u
                        t_mm = None
                        for f8 in range(8):
                            Fc = qtr * 8 + f8
                            w0 = []
                            if f8 == 0:
                                w0 = [wt, t_hw_tr[qtr * 2][tau]]
                                if qtr == 0:
                                    w0 += [pb_free[bank]]
                            elif f8 == 4:
                                w0 = [t_hw_tr[qtr * 2 + 1][tau]]
                            last_q = (u == 3 and f8 == 7)
                            t_mm = E("tensor", lambda h, s=slot, f=Fc, t=tau, c2=ch,
                                     bk=bank, f8_=f8: h.matmul(
                                out=pb[bk][:],
                                lhsT=rqhw[:, f * GT + t * 128:f * GT + (t + 1) * 128],
                                rhs=s[:, f8_ * 1024 + c2 * 512:f8_ * 1024 + (c2 + 1) * 512],
                                start=(f == 0), stop=(f == 31)),
                                waits=w0, inc=(SWF if last_q else SPE))
                            if last_q:
                                free_tok[gl] = t_mm
                        if qtr == 3:
                            t_ev = E("vector", lambda h, t=tau, c2=ch, bk=bank:
                                     h.tensor_copy(
                                out=ffn_tm[:, t * 1024 + c2 * 512:t * 1024 + (c2 + 1) * 512],
                                in_=pb[bk][:]), waits=[t_mm], inc=SDVE)
                            pb_free[bank] = t_ev
                            t_wo_ev[u >> 1][ch] = t_ev
                for ut in range(2):
                    tau = half * 2 + ut
                    t_n, t_r = ln_into(ffn_tm[:, tau * 1024:(tau + 1) * 1024], ntmp[:],
                                       tau % 2, [t_wo_ev[ut][0], t_wo_ev[ut][1]],
                                       res_x_ap=xcur[:, tau * 1024:(tau + 1) * 1024])
                    t_fin[tau] = t_r
            r0s = g * TPG
            x_store_tok[g] = E("scalar", lambda h, r0=r0s, xb=xcur: h.dma_start(
                out=xout.rearrange("(j p) f -> p j f", p=128)[:, r0:r0 + TPG, :],
                in_=xb.ap().rearrange("p (j f) -> p j f", j=TPG)),
                waits=[t_fin[2], t_fin[3]], inc=SOUT, inc_by=16)

            lnfm_free = free_tok[wislots[3]]
            pump_loads((g + 1) * LPG + NSLOT)

        kb.W("sync", [SOUT.tok()])

        block = es.enter_context(nc.Block())

        def mk(engname):
            def _(h):
                for eng, fn in kb.ops:
                    if eng == engname:
                        fn(h)
            return _

        block.tensor(mk("tensor"))
        block.vector(mk("vector"))
        block.scalar(mk("scalar"))
        block.gpsimd(mk("gpsimd"))
        block.sync(mk("sync"))

    return nc, kb.counts


def kernel(**inputs):
    import numpy as np
    x = np.ascontiguousarray(np.asarray(inputs["L0_attn_x"], dtype=np.float32))
    lay = host_layout({k: np.asarray(v) for k, v in inputs.items()})
    nc, _ = build_core_program(ng=NG_FULL)
    in_maps = []
    for c in range(NCORE):
        m = {"xin": np.ascontiguousarray(
            x[c * BPC:(c + 1) * BPC].reshape(BPC * T, C))}
        m.update(lay)
        in_maps.append(m)
    from concourse.bass_utils import run_bass_kernel_spmd
    res = run_bass_kernel_spmd(nc, in_maps, core_ids=list(range(NCORE)))
    out = np.empty((B, T, C), dtype=np.float32)
    for c in range(NCORE):
        out[c * BPC:(c + 1) * BPC] = np.asarray(
            res.results[c]["xout"]).reshape(BPC, T, C)
    return out


# version 13 (final)
# speedup vs baseline: 1.1875x; 1.1875x over previous
import contextlib
import numpy as np

import concourse.bass as bass
import concourse.mybir as mybir
from concourse.alu_op_type import AluOpType as ALU

F32 = mybir.dt.float32
BF16 = mybir.dt.bfloat16
AF = mybir.ActivationFunctionType

B, T, C = 16, 4096, 1024
NCORE = 8
BPC = B // NCORE
H, DH, HID, FF, NE = 16, 64, 256, 4096, 4
GT = 512
TPG = GT // 128
GPB = T // GT
NG_FULL = BPC * GPB
LN_EPS = 1e-5
RMS_EPS = float(np.finfo(np.float32).eps)
NSLOT = 2


class Sem:
    __slots__ = ("h", "n", "full")

    def __init__(self, h, full=False):
        self.h = h
        self.n = 0
        self.full = full

    def tok(self):
        return (self, self.n)


class KB:
    def __init__(self, nc):
        self.nc = nc
        self.ops = []
        self.counts = {k: 0 for k in ("tensor", "vector", "scalar", "gpsimd", "sync")}

    def E(self, eng, build, waits=(), inc=None, inc_by=1):
        toks = [(t[0].h, t[0].n if t[0].full else t[1]) for t in waits if t is not None]
        es_ = self.engsem.get(eng) if getattr(self, "simsafe", False) else None
        if es_ is not None and es_.n > 0:
            toks = [(es_.h, es_.n)] + toks
        ih = None
        if inc is not None:
            inc.n += inc_by
            ih = inc.h

        def fn(h, toks=toks, build=build, ih=ih, inc_by=inc_by):
            for sh, v in toks:
                h.wait_ge(sh, v)
            ins = build(h)
            if ih is not None:
                ins.then_inc(ih, inc_by)

        self.ops.append((eng, fn))
        self.counts[eng] += 1
        return inc.tok() if inc is not None else None

    def W(self, eng, waits):
        toks = [(t[0].h, t[0].n if t[0].full else t[1]) for t in waits if t is not None]

        def fn(h, toks=toks):
            for sh, v in toks:
                h.wait_ge(sh, v)

        self.ops.append((eng, fn))


def host_layout(inputs):
    f = np.ascontiguousarray

    def wT(Wx):
        o, c = Wx.shape
        t = Wx.T.reshape(c // 128, 128, o).transpose(1, 0, 2)
        return f(t.reshape(128, -1)).astype(np.float32)

    A = inputs["L0_ffn_A"]
    Bm = inputs["L0_ffn_B"]
    def wT_q(Wx):
        t = wT(Wx).reshape(128, 8, 4, 1024).transpose(0, 2, 1, 3)
        return f(t.reshape(128, -1)).astype(np.float32)

    d = {
        "wq_t": wT(inputs["L0_attn_Wq"]),
        "wk_t": wT(inputs["L0_attn_Wk"]),
        "wv_t": wT(inputs["L0_attn_Wv"]),
        "wm1_t": wT(inputs["L0_attn_Wm1"]),
        "wproj_t": wT(inputs["L0_attn_Wproj"]),
        "wfc_t": wT_q(inputs["L0_attn_Wfc"]),
        "wi_t": wT_q(inputs["L0_ffn_Wi"]),
        "wcp_t": wT(inputs["L0_attn_Wcp"]),
        "wo_t": wT(inputs["L0_ffn_Wo"]),
        "wm2_t": wT(inputs["L0_attn_Wm2"]),
    }
    d["wg_t"] = wT(inputs["L0_ffn_Wg"])
    d["a_t"] = wT(A.transpose(1, 0, 2).reshape(NE * 4, C))
    b96 = np.zeros((100, 8 * 512), dtype=np.float32)
    for e in range(NE):
        for fb in range(8):
            s = e * 8 + fb
            b0, wn = 32 * (s % 4), s // 4
            b96[b0:b0 + 4, wn * 512:(wn + 1) * 512] = Bm[e, fb * 512:(fb + 1) * 512, :].T
    d["b_t"] = f(b96)
    d["bm1c"] = f(inputs["L0_attn_bm1"].reshape(2, 128).T).astype(np.float32)
    d["bm2c"] = f(inputs["L0_attn_bm2"].reshape(16, 1)).astype(np.float32)
    d["bgc"] = f(np.tile(inputs["L0_ffn_bg"].reshape(1, 4), (128, 1))).astype(np.float32)
    return d


WIN_SHAPES = [
    ("wq_t", [128, 8 * 1024]), ("wk_t", [128, 8 * 1024]), ("wv_t", [128, 8 * 1024]),
    ("wm1_t", [128, 8 * 256]), ("wproj_t", [128, 8 * 1024]),
    ("wfc_t", [128, 8 * 4096]), ("wi_t", [128, 8 * 4096]),
    ("wcp_t", [128, 32 * 1024]), ("wo_t", [128, 32 * 1024]),
    ("wm2_t", [128, 2 * 16]), ("wg_t", [128, 8 * 4]), ("a_t", [128, 8 * 16]), ("b_t", [100, 8 * 512]),
    ("bm1c", [128, 2]), ("bm2c", [16, 1]), ("bgc", [128, 4]),
]
STREAMED = ["wq_t", "wk_t", "wv_t", "wm1_t", "wproj_t", "wfc_t", "wcp_t", "wi_t", "wo_t"]

TAP_SHAPES = {
    "ln1": [128, TPG * 1024], "qf": [128, 8 * GT], "kf": [128, 8 * GT],
    "vt": [128, TPG * 1024], "z": [16, GT], "ld": [128, 64], "pall": [128, 64],
    "yt": [128, TPG * 1024], "y2": [128, TPG * 1024], "yn": [128, TPG * 1024],
    "rsq": [128, 4 * GT], "mlp": [128, TPG * 1024],
    "xmid": [128, TPG * 1024], "w4": [128, 4], "lowt": [4, 4 * 128],
    "hwt": [128, 4 * 512], "ffn": [128, TPG * 1024],
    "sc_D": [128, 128], "sc_Am": [128, 128], "sc_Kg": [128, 64],
    "sc_kT": [128, 64], "sc_gC": [128, 1],
    "sc_Yin": [128, 64], "sc_S": [128, 256], "sc_P64": [128, 1],
}


def build_core_program(ng=NG_FULL, taps=None, stage=5, scan_limit=None):
    taps = taps or {}
    nc = bass.Bass("TRN2", target_bir_lowering=False)

    xin = nc.dram_tensor("xin", [BPC * T, C], F32, kind="ExternalInput").ap()
    xout = nc.dram_tensor("xout", [BPC * T, C], F32, kind="ExternalOutput").ap()
    win = {n: nc.dram_tensor(n, s, F32, kind="ExternalInput").ap() for n, s in WIN_SHAPES}
    wsc = {n: nc.dram_tensor("sc_" + n, list(win[n].tensor.shape), BF16, kind="Internal").ap()
           for n in STREAMED}
    tap_out = {t: nc.dram_tensor("tap_" + t, TAP_SHAPES[t], F32, kind="ExternalOutput").ap()
               for t in taps}

    kb = KB(nc)
    with contextlib.ExitStack() as es:
        def sb(name, shape, dt):
            return es.enter_context(nc.sbuf_tensor(name, shape, dt))

        def psum(name, shape, dt):
            return es.enter_context(nc.psum_tensor(name, shape, dt))

        def sem(name, full=False):
            return Sem(es.enter_context(nc.semaphore(name)), full)

        ones128b = sb("ones128b", [128, 128], BF16)
        identb = sb("identb", [128, 128], BF16)
        ident32 = sb("ident32", [128, 128], F32)
        tri32 = sb("tri32", [128, 128], F32)
        onesrow32 = sb("onesrow32", [1, 128], F32)
        wm2b = sb("wm2b", [128, 32], BF16)
        wgb = sb("wgb", [128, 8 * 4], BF16)
        atb = sb("atb", [128, 8 * 16], BF16)
        btb = sb("btb", [100, 8 * 512], BF16)
        bm1c = sb("bm1c_s", [128, 2], F32)
        bm2c = sb("bm2c_s", [16, 1], F32)
        bgc = sb("bgc_s", [128, 4], F32)
        wslot = [sb(f"wslot{i}", [128, 8192], BF16) for i in range(NSLOT)]
        xsb = [sb(f"xsb{i}", [128, TPG * 1024], F32) for i in range(2)]
        ln_tm = sb("ln_tm", [128, TPG * 1024], BF16)
        ln_fm = sb("ln_fm", [128, 8 * GT], BF16)
        q_fm = sb("q_fm", [128, 8 * GT], BF16)
        k_fm = sb("k_fm", [128, 8 * GT], BF16)
        v_tm = sb("v_tm", [128, TPG * 1024], BF16)

        z_sb = sb("z_sb", [16, GT], F32)
        z_tm = sb("z_tm", [128, 64], F32)
        u_tm = sb("u_tm", [128, 64], F32)
        t_tm = sb("t_tm", [128, 64], F32)
        w1_tm = sb("w1_tm", [128, 64], F32)
        ld_tm = sb("ld_tm", [128, 64], F32)
        e_tm = sb("e_tm", [128, 64], F32)
        g_tm = sb("g_tm", [128, 64], F32)
        P_all = sb("P_all", [128, 64], F32)
        PL_row = sb("PL_row", [1, 64], F32)
        y_tm = sb("y_tm", [128, TPG * 1024], BF16)
        y_fmb = [sb(f"y_fm{i}", [128, 8 * 128], BF16) for i in range(2)]
        m1r = y_fmb[0]
        y2_sb = sb("y2_sb", [128, TPG * 1024], BF16)
        trash = sb("trash", [128, 1024], BF16)
        ntmp = trash
        relub = trash
        bnst = [sb(f"bnst{i}", [128, 12], F32) for i in range(2)]
        mv = [sb(f"mv{i}", [128, 2], F32) for i in range(2)]
        rstd = [sb(f"rstd{i}", [128, 1], F32) for i in range(2)]
        nmr = [sb(f"nmr{i}", [128, 1], F32) for i in range(2)]
        Dsb = [sb(f"Dsb{i}", [128, 512], F32) for i in range(2)]
        Drep = [sb(f"Drep{i}", [128, 512], F32) for i in range(2)]
        Am = [sb(f"Am{i}", [128, 512], BF16) for i in range(2)]
        KgT = [sb(f"KgT{i}", [128, 64], BF16) for i in range(2)]
        Yin = [sb(f"Yin{i}", [128, 64], BF16) for i in range(2)]
        Ysc = [sb(f"Ysc{i}", [128, 64], BF16) for i in range(2)]
        gC = [sb(f"gC{i}", [128, 1], F32) for i in range(4)]
        P_bc = sb("P_bc", [128, 64], F32)
        S_sb = sb("S_sb", [128, (H // 2) * 64], BF16)
        epsc = sb("epsc", [128, 2], F32)
        lg = sb("lg", [128, 4], F32)
        gt1 = sb("gt1", [128, 12], F32)
        exs = sb("exs", [128, 4], F32)
        ssum = sb("ssum", [128, 1], F32)
        rs_g = sb("rs_g", [128, 1], F32)
        w4 = sb("w4", [128, 4 * 4], F32)
        lowT = sb("lowT", [100, TPG * 512], BF16)
        low_fm = sb("low_fm", [16, GT], BF16)
        basebuf = [sb(f"base{i}", [128, TPG * 512], BF16) for i in range(2)]
        hwbuf = basebuf
        h4 = [sb(f"h4_{i}", [128, 4 * 512], BF16) for i in range(2)]
        rqhw = sb("rqhw", [128, 32 * GT], BF16)

        pb = [psum(f"pb{i}", [128, 512], F32) for i in range(7)]
        pk = psum("pk", [128, 1024], BF16)

        SPE = sem("SPE"); SDVE = sem("SDVE"); SACT = sem("SACT"); SPOOL = sem("SPOOL")
        S_LNTR = sem("S_LNTR", full=True)
        S_YTR = sem("S_YTR", full=True)
        S_PL = sem("S_PL", full=True)
        S_LOWT = sem("S_LOWT", full=True)
        S_MLPTR = sem("S_MLPTR", full=True)
        S_HWTR = sem("S_HWTR", full=True)
        S_FFNTR = sem("S_FFNTR", full=True)
        SWSL = [sem(f"SWSL{i}", full=True) for i in range(NSLOT)]
        SX = [sem(f"SX{i}", full=True) for i in range(2)]
        SBIAS = sem("SBIAS", full=True)
        SRES = sem("SRES", full=True)
        SCAST = {n: sem("SC_" + n, full=True) for n in STREAMED}
        SWF = sem("SWF")
        SOUT = sem("SOUT", full=True)
        kb.engsem = {"vector": SDVE, "scalar": SACT, "gpsimd": SPOOL}
        import os as _os3
        kb.simsafe = bool(int(_os3.environ.get("BASS_SIMSAFE", "0")))

        E = kb.E

        NONE = None

        t_iota0 = E("gpsimd", lambda h: h.iota(Dsb[0][:, 0:128], pattern=[[0, 128]],
                                               channel_multiplier=1,
                                               allow_small_or_imprecise_dtypes=True), inc=SPOOL)
        t_iota = E("gpsimd", lambda h: h.iota(Dsb[1][:, 0:128], pattern=[[1, 128]],
                                              channel_multiplier=0,
                                              allow_small_or_imprecise_dtypes=True), inc=SPOOL)
        E("vector", lambda h: h.memset(ones128b[:], 1.0), inc=SDVE)
        E("vector", lambda h: h.memset(epsc[:, 0:1], LN_EPS), inc=SDVE)
        E("vector", lambda h: h.memset(epsc[:, 1:2], RMS_EPS), inc=SDVE)
        E("vector", lambda h: h.memset(onesrow32[:], 1.0), inc=SDVE)
        t_memset = E("vector", lambda h: h.memset(S_sb[:], 0.0), inc=SDVE)
        t_id32 = E("vector", lambda h: h.tensor_tensor(out=ident32[:], in0=Dsb[0][:, 0:128],
                                                       in1=Dsb[1][:, 0:128],
                                                       op=ALU.is_equal),
                   waits=[t_iota, t_memset], inc=SDVE)
        E("vector", lambda h: h.tensor_tensor(out=tri32[:], in0=Dsb[0][:, 0:128], in1=Dsb[1][:, 0:128],
                                              op=ALU.is_le), inc=SDVE)
        t_consts = E("vector", lambda h: h.tensor_copy(out=identb[:], in_=ident32[:]),
                     waits=[t_id32], inc=SDVE)
        E("gpsimd", lambda h: h.dma_start(out=wm2b[:], in_=win["wm2_t"]), inc=SRES, inc_by=16)
        E("gpsimd", lambda h: h.dma_start(out=wgb[:], in_=win["wg_t"]), inc=SRES, inc_by=16)
        E("gpsimd", lambda h: h.dma_start(out=atb[:], in_=win["a_t"]), inc=SRES, inc_by=16)
        t_smallw = E("gpsimd", lambda h: h.dma_start(out=btb[:], in_=win["b_t"]),
                     inc=SRES, inc_by=16)
        E("scalar", lambda h: h.dma_start(out=bm1c[:], in_=win["bm1c"]), inc=SBIAS, inc_by=16)
        E("scalar", lambda h: h.dma_start(out=bm2c[:], in_=win["bm2c"]), inc=SBIAS, inc_by=16)
        t_bias = E("scalar", lambda h: h.dma_start(out=bgc[:], in_=win["bgc"]),
                   inc=SBIAS, inc_by=16)
        cast_tok = {}
        for name in STREAMED:
            cast_tok[name] = E("gpsimd", lambda h, n=name: h.dma_start(out=wsc[n], in_=win[n]),
                               inc=SCAST[name], inc_by=16)
        x_tok = [None] * (ng + 1)
        x_tok[0] = E("scalar", lambda h: h.dma_start(
            out=xsb[0].ap().rearrange("p (j f) -> p j f", j=TPG),
            in_=xin.rearrange("(j p) f -> p j f", p=128)[:, 0:TPG, :]),
            inc=SX[0], inc_by=16)
        init_waits = [t_consts, t_memset, t_smallw, t_bias]

        group_loads = (
            [("wq_t", 0, 8192), ("wk_t", 0, 8192), ("wv_t", 0, 8192),
             ("wm1_t", 0, 2048), ("wproj_t", 0, 8192)]
            + [("wfc_t", q * 8192, 8192) for q in range(4)]
            + [("wcp_t", q * 8192, 8192) for q in range(4)]
            + [("wcp_t", q * 8192, 8192) for q in range(4)]
            + [("wi_t", q * 8192, 8192) for q in range(4)]
            + [("wo_t", q * 8192, 8192) for q in range(4)]
            + [("wo_t", q * 8192, 8192) for q in range(4)]
        )
        LPG = len(group_loads)
        load_tok = {}
        free_tok = {}

        def emit_wload(gl):
            name, ofs, ncols = group_loads[gl % LPG]
            slot = wslot[gl % NSLOT]
            waits = [cast_tok[name]]
            if gl >= NSLOT:
                waits.append(free_tok[gl - NSLOT])
            load_tok[gl] = E("gpsimd", lambda h, n=name, o=ofs, k=ncols, s=slot:
                             h.dma_start(out=s[:, 0:k], in_=wsc[n][:, o:o + k]),
                             waits=waits, inc=SWSL[gl % NSLOT], inc_by=16)

        next_load = [0]

        def pump_loads(upto):
            while next_load[0] < min(upto, ng * LPG):
                gl = next_load[0]
                if gl - NSLOT >= 0 and (gl - NSLOT) not in free_tok:
                    break
                emit_wload(gl)
                next_load[0] += 1

        pump_loads(NSLOT)

        def get_load(gl):
            pump_loads(gl + 1)
            assert gl in load_tok, f"load {gl} not emittable (missing free)"
            return load_tok[gl]

        pb_free = [None] * 7
        pk_free = NONE
        lnfm_free = None
        lntm_free = None
        x_store_tok = [None] * ng
        yfm_free = [None, None]
        h4_free = [None, None]
        base_free = [None, None]
        hw_free = [None, None]
        scan_state_tok = {}
        S_ready = t_memset
        w4_free = None
        lowT_free = None

        def tap(name, g_req, g, src_ap, wtok):
            if name in taps and taps[name] == g_req and g == g_req:
                E("gpsimd", lambda h: h.dma_start(out=tap_out[name], in_=src_ap),
                  waits=[wtok], inc=SOUT, inc_by=16)

        def ln_into(a_ap, out_ap, par, ptoks, res_x_ap=None, extra=()):
            bn = bnst[par]; m_ = mv[par]; r_ = rstd[par]; n_ = nmr[par]
            E("vector", lambda h: h.bn_stats(out=bn[:, 0:6], in_=a_ap[:, 0:512]),
              waits=[*ptoks, *extra], inc=SDVE)
            t_bn2 = E("vector", lambda h: h.bn_stats(out=bn[:, 6:12], in_=a_ap[:, 512:1024]),
                      inc=SDVE)
            t_mv = E("vector", lambda h: h.bn_aggr(out=m_[:], in_=bn[:]), waits=[t_bn2],
                     inc=SDVE)
            E("scalar", lambda h: h.activation(out=r_[:], in_=m_[:, 1:2], func=AF.Ln,
                                               bias=epsc[:, 0:1]), waits=[t_mv], inc=SACT)
            t_r = E("scalar", lambda h: h.activation(out=r_[:], in_=r_[:], func=AF.Exp,
                                                     scale=-0.5), inc=SACT)
            t_nmr = E("vector", lambda h: h.tensor_scalar(out=n_[:], in0=m_[:, 0:1],
                                                          scalar1=r_[:], scalar2=-1.0,
                                                          op0=ALU.mult, op1=ALU.mult),
                      waits=[t_r], inc=SDVE)
            t_norm = E("vector", lambda h: h.tensor_scalar(out=out_ap, in0=a_ap,
                                                           scalar1=r_[:], scalar2=n_[:],
                                                           op0=ALU.mult, op1=ALU.add),
                       waits=[t_nmr], inc=SDVE)
            t_res = None
            if res_x_ap is not None:
                t_res = E("vector", lambda h: h.tensor_tensor(out=res_x_ap, in0=res_x_ap,
                                                              in1=out_ap, op=ALU.add),
                          inc=SDVE)
            return t_norm, t_res

        LNFM3 = ln_fm.ap().rearrange("p (j f) -> p j f", j=8)
        RQ3 = rqhw.ap().rearrange("p (q f) -> p q f", q=32)

        for g in range(ng):
            base_l = g * LPG
            xcur = xsb[g % 2]
            xnxt = xsb[(g + 1) % 2]
            if g + 1 < ng:
                r0 = (g + 1) * TPG
                ws = [x_store_tok[g - 1]] if g >= 1 else []
                x_tok[g + 1] = E("scalar", lambda h, r0=r0, xb=xnxt: h.dma_start(
                    out=xb.ap().rearrange("p (j f) -> p j f", j=TPG),
                    in_=xin.rearrange("(j p) f -> p j f", p=128)[:, r0:r0 + TPG, :]),
                    waits=ws, inc=SX[(g + 1) % 2], inc_by=16)
            if g == GPB:
                wS = [scan_state_tok[hh] for hh in range(H)]
                S_ready = E("vector", lambda h: h.memset(S_sb[:], 0.0), waits=wS, inc=SDVE)
                scan_state_tok = {}

            t_ln1 = []
            for tau in range(TPG):
                t_n, _ = ln_into(xcur[:, tau * 1024:(tau + 1) * 1024],
                                 ln_tm[:, tau * 1024:(tau + 1) * 1024], tau % 2,
                                 [x_tok[g]], extra=[lntm_free, *init_waits])
                t_ln1.append(E("sync", lambda h, t=tau: h.dma_start(
                    out=LNFM3[:, :, t * 128:(t + 1) * 128],
                    in_=ln_tm[:, t * 1024:(t + 1) * 1024], transpose=True),
                    waits=[t_n, lnfm_free], inc=S_LNTR, inc_by=16))
            t_ln1_all = t_ln1[-1]
            tap("ln1", taps.get("ln1"), g, ln_tm[:], t_ln1_all)

            tq = get_load(base_l + 0)
            tk = get_load(base_l + 1)
            qk_done = {}
            for idx, (nm, wtok, dst, eng_e) in enumerate(
                    [("q", tq, q_fm, "vector"), ("k", tk, k_fm, "scalar")]):
                slot = wslot[(base_l + idx) % NSLOT]
                for och in range(8):
                    bank = och % 4
                    t_mm = None
                    for c in range(8):
                        waits = [wtok, t_ln1_all, pb_free[bank]] if c == 0 else []
                        last = (och == 7 and c == 7)
                        t_mm = E("tensor", lambda h, s=slot, cc=c, o=och, bk=bank:
                                 h.matmul(out=pb[bk][:],
                                          lhsT=s[:, cc * 1024 + o * 128:cc * 1024 + (o + 1) * 128],
                                          rhs=ln_fm[:, cc * GT:(cc + 1) * GT],
                                          start=(cc == 0), stop=(cc == 7)),
                                 waits=waits, inc=(SWF if last else SPE))
                        if last:
                            free_tok[base_l + idx] = t_mm
                    if eng_e == "vector":
                        pb_free[bank] = E("vector", lambda h, o=och, bk=bank, d=dst:
                                          h.tensor_copy(out=d[:, o * GT:(o + 1) * GT],
                                                        in_=pb[bk][:]),
                                          waits=[t_mm], inc=SDVE)
                    else:
                        pb_free[bank] = E("scalar", lambda h, o=och, bk=bank, d=dst:
                                          h.activation(out=d[:, o * GT:(o + 1) * GT],
                                                       in_=pb[bk][:], func=AF.Copy),
                                          waits=[t_mm], inc=SACT)
                qk_done[nm] = pb_free[bank]
            t_qdone = qk_done["q"]; t_kdone = qk_done["k"]
            tap("qf", taps.get("qf"), g, q_fm[:], t_qdone)
            tap("kf", taps.get("kf"), g, k_fm[:], t_kdone)

            tv = get_load(base_l + 2)
            vslot = wslot[(base_l + 2) % NSLOT]
            t_vdone = None
            for tau in range(TPG):
                bank = tau % 2
                t_mm = None
                for c in range(8):
                    w0 = [tv, t_ln1[tau], pb_free[bank], pb_free[2 + bank]] if c == 0 else []
                    last = (tau == TPG - 1 and c == 7)
                    E("tensor", lambda h, s=vslot, cc=c, t=tau, bk=bank:
                      h.matmul(out=pb[bk][:],
                               lhsT=ln_fm[:, cc * GT + t * 128:cc * GT + (t + 1) * 128],
                               rhs=s[:, cc * 1024:cc * 1024 + 512],
                               start=(cc == 0), stop=(cc == 7)), waits=w0, inc=SPE)
                    t_mm = E("tensor", lambda h, s=vslot, cc=c, t=tau, bk=bank:
                             h.matmul(out=pb[2 + bk][:],
                                      lhsT=ln_fm[:, cc * GT + t * 128:cc * GT + (t + 1) * 128],
                                      rhs=s[:, cc * 1024 + 512:cc * 1024 + 1024],
                                      start=(cc == 0), stop=(cc == 7)),
                             inc=(SWF if last else SPE))
                    if last:
                        free_tok[base_l + 2] = t_mm
                pb_free[bank] = E("vector", lambda h, t=tau, bk=bank: h.tensor_copy(
                    out=v_tm[:, t * 1024:t * 1024 + 512], in_=pb[bk][:]),
                    waits=[t_mm], inc=SDVE)
                t_vdone = E("vector", lambda h, t=tau, bk=bank: h.tensor_copy(
                    out=v_tm[:, t * 1024 + 512:(t + 1) * 1024], in_=pb[2 + bk][:]), inc=SDVE)
                pb_free[2 + bank] = t_vdone
            tap("vt", taps.get("vt"), g, v_tm[:], t_vdone)

            tm1 = get_load(base_l + 3)
            m1slot = wslot[(base_l + 3) % NSLOT]
            t_m1 = None
            for oc in range(2):
                bank = 4 + oc
                t_mm = None
                for c in range(8):
                    w0 = [tm1, t_ln1_all, pb_free[bank]] if c == 0 else []
                    last = (oc == 1 and c == 7)
                    t_mm = E("tensor", lambda h, s=m1slot, cc=c, o=oc, bk=bank:
                             h.matmul(out=pb[bk][:],
                                      lhsT=s[:, cc * 256 + o * 128:cc * 256 + (o + 1) * 128],
                                      rhs=ln_fm[:, cc * GT:(cc + 1) * GT],
                                      start=(cc == 0), stop=(cc == 7)),
                             waits=w0, inc=(SWF if last else SPE))
                    if last:
                        free_tok[base_l + 3] = t_mm
                t_m1 = E("scalar", lambda h, o=oc, bk=bank: h.activation(
                    out=m1r[:, o * GT:(o + 1) * GT], in_=pb[bk][:], func=AF.Relu,
                    bias=bm1c[:, o:o + 1]), waits=[t_mm], inc=SACT)
                pb_free[bank] = t_m1
            t_zmm = None
            for hc in range(2):
                t_zmm = E("tensor", lambda h, c=hc: h.matmul(
                    out=pb[6][0:16, 0:GT], lhsT=wm2b[:, c * 16:(c + 1) * 16],
                    rhs=m1r[:, c * GT:(c + 1) * GT], start=(c == 0), stop=(c == 1)),
                    waits=([t_m1, pb_free[6], t_smallw] if hc == 0 else []), inc=SPE)
            t_zsb = E("vector", lambda h: h.tensor_scalar(
                out=z_sb[:], in0=pb[6][0:16, 0:GT], scalar1=bm2c[:], scalar2=None,
                op0=ALU.add), waits=[t_zmm], inc=SDVE)
            pb_free[6] = t_zsb
            tap("z", taps.get("z"), g, z_sb[:], t_zsb)
            t_ztr = None
            for tau in range(TPG):
                t_ztr = E("tensor", lambda h, t=tau: h.transpose(
                    out=pb[6][:, t * 16:(t + 1) * 16],
                    in_=z_sb[0:16, t * 128:(t + 1) * 128],
                    identity=ident32[0:16, 0:16]),
                    waits=([t_zsb] if tau == 0 else []), inc=SPE)
            t_ztm = E("vector", lambda h: h.tensor_copy(out=z_tm[:], in_=pb[6][:, 0:64]),
                      waits=[t_ztr], inc=SDVE)
            pb_free[6] = t_ztm
            E("scalar", lambda h: h.activation(out=u_tm[:], in_=z_tm[:], func=AF.Abs),
              waits=[t_ztm], inc=SACT)
            E("scalar", lambda h: h.activation(out=t_tm[:], in_=u_tm[:], func=AF.Exp,
                                               scale=-1.0), inc=SACT)
            E("scalar", lambda h: h.activation(out=w1_tm[:], in_=t_tm[:], func=AF.Ln,
                                               bias=1.0), inc=SACT)
            t_sp = E("scalar", lambda h: h.activation(out=u_tm[:], in_=z_tm[:],
                                                      func=AF.Relu), inc=SACT)
            t_ld = E("vector", lambda h: h.scalar_tensor_tensor(
                out=ld_tm[:], in0=u_tm[:], scalar=-1.0, in1=w1_tm[:],
                op0=ALU.mult, op1=ALU.subtract), waits=[t_sp], inc=SDVE)
            t_e = E("scalar", lambda h: h.activation(out=e_tm[:], in_=ld_tm[:],
                                                     func=AF.Exp), waits=[t_ld], inc=SACT)
            t_g = E("vector", lambda h: h.tensor_scalar(
                out=g_tm[:], in0=e_tm[:], scalar1=-1.0, scalar2=1.0,
                op0=ALU.mult, op1=ALU.add), waits=[t_e], inc=SDVE)
            tap("ld", taps.get("ld"), g, ld_tm[:], t_ld)
            t_lam = E("tensor", lambda h: h.matmul(
                out=pb[6][:, 64:128], lhsT=tri32[:], rhs=ld_tm[:],
                start=True, stop=True), waits=[t_ld, pb_free[6]], inc=SPE)
            t_P = E("scalar", lambda h: h.activation(out=P_all[:], in_=pb[6][:, 64:128],
                                                     func=AF.Exp), waits=[t_lam], inc=SACT)
            pb_free[6] = t_P
            t_PL = E("sync", lambda h: h.dma_start(out=PL_row[:], in_=P_all[127:128, :]),
                     waits=[t_P], inc=S_PL, inc_by=16)
            t_pmm = E("tensor", lambda h: h.matmul(
                out=pb[6][:, 0:64], lhsT=onesrow32[0:1, 0:128], rhs=PL_row[0:1, 0:64],
                start=True, stop=True), waits=[t_PL, pb_free[6]], inc=SPE)
            t_pbc = E("vector", lambda h: h.tensor_copy(out=P_bc[:], in_=pb[6][:, 0:64]),
                      waits=[t_pmm], inc=SDVE)
            pb_free[6] = t_pbc
            tap("pall", taps.get("pall"), g, P_all[:], t_P)
            lnfm_free = free_tok[base_l + 3]
            lntm_free = t_ln1_all

            if stage < 2:
                continue
            _ho = list(range(0, H, 2)) + list(range(1, H, 2))
            chead = [(tau, hh) for tau in range(TPG) for hh in _ho]
            NCH = len(chead)
            NB = NCH // 4
            tok_batch_pe = [None] * NB
            tok_batch_dve = [None] * NB
            tok_kgt = [None] * NCH
            tok_yin = [None] * NCH
            tok_kt = [None] * NCH
            tok_yT = [None] * NCH
            tok_sup = [None] * NCH
            tokD = [None] * NCH

            def q_sl(tau, hh):
                oc = hh >> 1; bp = 64 * (hh & 1)
                return q_fm[bp:bp + 64, oc * GT + tau * 128:oc * GT + (tau + 1) * 128]

            def k_sl(tau, hh):
                oc = hh >> 1; bp = 64 * (hh & 1)
                return k_fm[bp:bp + 64, oc * GT + tau * 128:oc * GT + (tau + 1) * 128]

            def v_sl(tau, hh):
                return v_tm[:, tau * 1024 + hh * 64:tau * 1024 + (hh + 1) * 64]

            def S_sl(hh):
                bp = 64 * (hh & 1)
                return S_sb[bp:bp + 64, (hh >> 1) * 64:((hh >> 1) + 1) * 64]

            NCH_RUN = NCH if scan_limit is None else min(scan_limit, NCH)
            NB_RUN = max(1, NCH_RUN // 4)

            def emit_batchA(b):
                par = b % 2
                bD = 3 if par == 0 else 0
                bA = 2 if par == 0 else 1
                tp = None
                for u in range(4):
                    tau, hh = chead[4 * b + u]
                    cl = tau * 16 + hh
                    wA = [t_g]
                    if u == 0 and b >= 2:
                        wA.append(tok_batch_pe[b - 2])
                    tp = E("gpsimd", lambda h, p=par, c=cl, u_=u: h.tensor_scalar(
                        out=Drep[p][:, u_ * 128:(u_ + 1) * 128], in0=ones128b[:],
                        scalar1=e_tm[:, c:c + 1], scalar2=1.0,
                        op0=ALU.mult, op1=ALU.mult), waits=wA, inc=SPOOL)
                import os as _os2
                _sp2 = int(_os2.environ.get("SCANPART", "3"))
                tt = None
                for u in range(4):
                    w = [tp, t_qdone, t_kdone] if u == 0 else []
                    if u == 0 and b >= 2:
                        w.append(tok_batch_dve[b - 2])
                    if u == 0 and b < 2:
                        w += [pb_free[bD], pb_free[bA]]
                    if _sp2 == 11:
                        tt = E("tensor", lambda h: h.matmul(
                            out=pb[6][0:1, 0:1], lhsT=onesrow32[0:1, 0:1],
                            rhs=onesrow32[0:1, 0:1], start=True, stop=True),
                            waits=w, inc=SPE)
                        continue
                    tt = E("tensor", lambda h, p=par, bd=bD, u_=u: h.transpose(
                        out=pb[bd][:, u_ * 128:(u_ + 1) * 128],
                        in_=Drep[p][:, u_ * 128:(u_ + 1) * 128],
                        identity=ident32[:]), waits=w, inc=SPE)
                for u in range(4):
                    tau, hh = chead[4 * b + u]
                    if _sp2 == 10:
                        continue
                    uu = 0 if _sp2 == 13 else u
                    if _sp2 == 15:
                        hh = 0
                    if _sp2 == 17:
                        hh = 1
                    if _sp2 == 16:
                        tt = E("tensor", lambda h, t=tau, h_=hh, u_=u: h.matmul(
                            out=pb[2][:, u_ * 128:(u_ + 1) * 128], lhsT=k_sl(t, h_),
                            rhs=q_sl(t, h_), start=True, stop=True), inc=SPE)
                        continue
                    tt = E("tensor", lambda h, t=tau, h_=hh, ba=bA, u_=uu: h.matmul(
                        out=pb[ba][:, u_ * 128:(u_ + 1) * 128], lhsT=k_sl(t, h_),
                        rhs=q_sl(t, h_), start=True, stop=True), inc=SPE)
                tok_batch_pe[b] = tt

            def emit_batchB(b):
                par = b % 2
                bD = 3 if par == 0 else 0
                bA = 2 if par == 0 else 1
                td = None
                for u in range(4):
                    w = [tok_batch_pe[b]] if u == 0 else []
                    td = E("vector", lambda h, p=par, bd=bD, u_=u: h.tensor_tensor_scan(
                        out=Dsb[p][:, u_ * 128:(u_ + 1) * 128],
                        data0=pb[bd][:, u_ * 128:(u_ + 1) * 128], data1=identb[:],
                        initial=0.0, op0=ALU.mult, op1=ALU.add), waits=w, inc=SDVE)
                ta = None
                for u in range(4):
                    tau, hh = chead[4 * b + u]
                    cl = tau * 16 + hh
                    w = []
                    if u == 0 and b >= 1:
                        if b >= 2:
                            w.append(tok_sup[4 * (b - 2) + 3])
                    ta = E("vector", lambda h, p=par, ba=bA, c=cl, u_=u:
                           h.scalar_tensor_tensor(
                        out=Am[p][:, u_ * 128:(u_ + 1) * 128],
                        in0=pb[ba][:, u_ * 128:(u_ + 1) * 128],
                        scalar=g_tm[:, c:c + 1], in1=Dsb[p][:, u_ * 128:(u_ + 1) * 128],
                        op0=ALU.mult, op1=ALU.mult), waits=w, inc=SDVE)
                for u in range(4):
                    tau, hh = chead[4 * b + u]
                    cl = tau * 16 + hh
                    E("scalar", lambda h, p=par, c=cl, u_=u, g2=(4 * b + u) % 4:
                      h.activation(
                        out=gC[g2][:], in_=Dsb[p][:, u_ * 128 + 127:u_ * 128 + 128],
                        func=AF.Copy, scale=g_tm[:, c:c + 1]),
                      waits=[ta] + ([tok_kgt[4 * b + u - 4]] if 4 * b + u >= 4 else []),
                      inc=SACT)
                tok_batch_dve[b] = SDVE.tok()

            def emit_headC(j):
                tau, hh = chead[j]
                if j == 0:
                    t0_, h0_ = chead[0]
                    b0_ = 64 * (h0_ & 1)
                    tok_kt[0] = E("tensor", lambda h, t=t0_, h_=h0_, bp=b0_: h.transpose(
                        out=pk[:, 0:64], in_=k_sl(t, h_),
                        identity=identb[bp:bp + 64, bp:bp + 64]),
                        waits=[pk_free], inc=SPE)
                par = j % 2
                b = j // 4
                u = j % 4
                bpar = b % 2
                bA = 2 if bpar == 0 else 1
                cl = tau * 16 + hh
                bp_h = 64 * (hh & 1)
                wkg = [tok_batch_dve[b], tok_kt[j]]
                if j >= 2:
                    wkg.append(tok_sup[j - 2])
                tok_kgt[j] = E("scalar", lambda h, p=par, g2=j % 4: h.activation(
                    out=KgT[p][:], in_=pk[:, p * 512:p * 512 + 64], func=AF.Copy,
                    scale=gC[g2][:]), waits=wkg, inc=SACT)
                bky = 4 + par
                wc = [tok_batch_dve[b], t_vdone, S_ready, t_PL]
                if hh in scan_state_tok:
                    wc.append(scan_state_tok[hh])
                if j >= 2:
                    wc += [tok_yT[j - 2]]
                else:
                    wc += [pb_free[bky]]
                t_yintra = E("tensor", lambda h, p=bpar, ba=bA, u_=u, t=tau, h_=hh,
                             bk=bky: h.matmul(
                    out=pb[bk][:, 0:64], lhsT=Am[p][:, u_ * 128:(u_ + 1) * 128],
                    rhs=v_sl(t, h_), start=True, stop=True), waits=wc, inc=SPE)
                t_yinter = E("tensor", lambda h, t=tau, h_=hh, bk=bky: h.matmul(
                    out=pb[bk][:, 64:128], lhsT=q_sl(t, h_),
                    rhs=S_sl(h_), start=True, stop=True), inc=SPE)
                if j + 1 < NCH_RUN:
                    tn, hn = chead[j + 1]
                    bpn = 64 * (hn & 1)
                    tok_kt[j + 1] = E("tensor", lambda h, t=tn, h_=hn, p=(j + 1) % 2,
                                      bp=bpn: h.transpose(
                        out=pk[:, p * 512:p * 512 + 64], in_=k_sl(t, h_),
                        identity=identb[bp:bp + 64, bp:bp + 64]),
                        waits=[tok_kgt[j]], inc=SPE)
                wsup = [tok_kgt[j]]
                wsup.append(tokD[j - 1] if j >= 1 else pb_free[6])
                tok_sup[j] = E("tensor", lambda h, p=par, t=tau, h_=hh, bp=bp_h: h.matmul(
                    out=pb[6][bp:bp + 64, 0:64], lhsT=KgT[p][:], rhs=v_sl(t, h_),
                    start=True, stop=True, tile_position=(0, bp)), waits=wsup, inc=SPE)
                tokC_pe = tok_sup[j]
                wyi = [t_yintra] + ([tok_yT[j - 2]] if j >= 2 else [])
                tok_yin[j] = E("scalar", lambda h, p=par, bk=bky: h.activation(
                    out=Yin[p][:], in_=pb[bk][:, 0:64], func=AF.Copy), waits=wyi, inc=SACT)
                tok_yT[j] = E("vector", lambda h, t=tau, h_=hh, c=cl, p=par, bk=bky:
                              h.scalar_tensor_tensor(
                    out=y_tm[:, t * 1024 + h_ * 64:t * 1024 + (h_ + 1) * 64],
                    in0=pb[bk][:, 64:128], scalar=P_all[:, c:c + 1],
                    in1=Yin[p][:], op0=ALU.mult, op1=ALU.add),
                    waits=[t_yinter, tok_yin[j]], inc=SDVE)
                tokD[j] = E("vector", lambda h, c=cl, h_=hh, bp=bp_h:
                            h.scalar_tensor_tensor(
                    out=S_sl(h_), in0=S_sl(h_), scalar=P_bc[bp:bp + 64, c:c + 1],
                    in1=pb[6][bp:bp + 64, 0:64], op0=ALU.mult, op1=ALU.add),
                    waits=[tok_sup[j]], inc=SDVE)
                scan_state_tok[hh] = tokD[j]

            for s in range(NB_RUN + 2):
                if s < NB_RUN:
                    emit_batchA(s)
                if s >= 2:
                    for u in range(4):
                        j = 4 * (s - 2) + u
                        if j < NCH_RUN:
                            emit_headC(j)
                if 1 <= s <= NB_RUN:
                    emit_batchB(s - 1)
            NL = NCH_RUN - 1
            t_scan = tokD[NL]
            pb_free[4 + (NL % 2)] = tok_yT[NL]
            pb_free[4 + ((NL - 1) % 2)] = tok_yT[NL - 1] if NL >= 1 else tok_yT[NL]
            pb_free[6] = tokD[NL]
            for bb in (0, 1, 2, 3):
                pb_free[bb] = tok_batch_dve[NB_RUN - 1]
            pk_free = tok_kgt[NL]
            tap("yt", taps.get("yt"), g, y_tm[:], tok_yT[NL])

            if stage < 3:
                continue
            tproj = get_load(base_l + 4)
            projslot = wslot[(base_l + 4) % NSLOT]
            t_yn_tr = []
            for tau in range(TPG):
                par = tau % 2
                bk0 = (2 * tau) % 4
                bk1 = (2 * tau + 1) % 4
                ty_tr = E("sync", lambda h, t=tau, p=par: h.dma_start(
                    out=y_fmb[p].ap().rearrange("p (j f) -> p j f", j=8),
                    in_=y_tm[:, t * 1024:(t + 1) * 1024], transpose=True),
                    waits=[tok_yT[tau * H + H - 1], yfm_free[par]], inc=S_YTR, inc_by=16)
                t_mm = None
                for c in range(8):
                    w0 = [ty_tr, tproj, pb_free[bk0], pb_free[bk1]] if c == 0 else []
                    last = (tau == TPG - 1 and c == 7)
                    E("tensor", lambda h, s=projslot, cc=c, p=par, b=bk0: h.matmul(
                        out=pb[b][:], lhsT=y_fmb[p][:, cc * 128:(cc + 1) * 128],
                        rhs=s[:, cc * 1024:cc * 1024 + 512],
                        start=(cc == 0), stop=(cc == 7)), waits=w0, inc=SPE)
                    t_mm = E("tensor", lambda h, s=projslot, cc=c, p=par, b=bk1: h.matmul(
                        out=pb[b][:], lhsT=y_fmb[p][:, cc * 128:(cc + 1) * 128],
                        rhs=s[:, cc * 1024 + 512:cc * 1024 + 1024],
                        start=(cc == 0), stop=(cc == 7)), inc=(SWF if last else SPE))
                    if last:
                        free_tok[base_l + 4] = t_mm
                yfm_free[par] = t_mm
                pb_free[bk0] = E("vector", lambda h, t=tau, b=bk0: h.tensor_copy(
                    out=y2_sb[:, t * 1024:t * 1024 + 512], in_=pb[b][:]),
                    waits=[t_mm], inc=SDVE)
                t_y2e = E("vector", lambda h, t=tau, b=bk1: h.tensor_copy(
                    out=y2_sb[:, t * 1024 + 512:(t + 1) * 1024], in_=pb[b][:]), inc=SDVE)
                pb_free[bk1] = t_y2e
                t_sqacc = E("scalar", lambda h, t=tau, p=par: h.activation(
                    out=trash[:], in_=y2_sb[:, t * 1024:(t + 1) * 1024], func=AF.Square,
                    accum_out=rstd[p][:]), waits=[t_y2e], inc=SACT)
                E("scalar", lambda h, p=par: h.activation(
                    out=rstd[p][:], in_=rstd[p][:], func=AF.Ln, scale=1.0 / 1024.0,
                    bias=epsc[:, 1:2]), waits=[t_sqacc], inc=SACT)
                t_rs = E("scalar", lambda h, p=par: h.activation(
                    out=rstd[p][:], in_=rstd[p][:], func=AF.Exp, scale=-0.5), inc=SACT)
                t_yn = E("vector", lambda h, t=tau, p=par: h.tensor_scalar(
                    out=ln_tm[:, t * 1024:(t + 1) * 1024],
                    in0=y2_sb[:, t * 1024:(t + 1) * 1024], scalar1=rstd[p][:],
                    scalar2=None, op0=ALU.mult), waits=[t_rs, lntm_free], inc=SDVE)
                t_yn_tr.append(E("sync", lambda h, t=tau: h.dma_start(
                    out=LNFM3[:, :, t * 128:(t + 1) * 128],
                    in_=ln_tm[:, t * 1024:(t + 1) * 1024], transpose=True),
                    waits=[t_yn, lnfm_free], inc=S_LNTR, inc_by=16))
            t_yn_all = t_yn_tr[-1]
            tap("y2", taps.get("y2"), g, y2_sb[:], pb_free[1])
            tap("yn", taps.get("yn"), g, ln_tm[:], SDVE.tok())
            lntm_free = t_yn_all

            if stage < 4:
                continue
            t_sq = [None, None]
            t_sq_all = [None] * 32
            t_fc_last = None
            for Fc in range(32):
                qtr = Fc // 8
                slot = wslot[(base_l + 5 + qtr) % NSLOT]
                wt = get_load(base_l + 5 + qtr)
                bank = Fc % 4
                t_mm = None
                for c in range(8):
                    w0 = [wt, t_yn_all, pb_free[bank]] if c == 0 else []
                    last = (Fc % 8 == 7 and c == 7)
                    t_mm = E("tensor", lambda h, s=slot, cc=c, f=Fc, bk=bank: h.matmul(
                        out=pb[bk][:],
                        lhsT=s[:, cc * 1024 + (f % 8) * 128:cc * 1024 + (f % 8 + 1) * 128],
                        rhs=ln_fm[:, cc * GT:(cc + 1) * GT],
                        start=(cc == 0), stop=(cc == 7)),
                        waits=w0, inc=(SWF if last else SPE))
                    if last:
                        free_tok[base_l + 5 + qtr] = t_mm
                wr = [t_mm] + ([t_sq[Fc % 2]] if t_sq[Fc % 2] is not None else [])
                t_rel = E("vector", lambda h, bk=bank, f=Fc: h.tensor_scalar(
                    out=relub[:, (f % 2) * 512:(f % 2 + 1) * 512], in0=pb[bk][:],
                    scalar1=0.0, scalar2=None, op0=ALU.max), waits=wr, inc=SDVE)
                pb_free[bank] = t_rel
                t_sq[Fc % 2] = E("scalar", lambda h, f=Fc: h.activation(
                    out=rqhw[:, f * GT:(f + 1) * GT],
                    in_=relub[:, (f % 2) * 512:(f % 2 + 1) * 512], func=AF.Square),
                    waits=[t_rel], inc=SACT)
                t_sq_all[Fc] = t_sq[Fc % 2]
            t_rsq0, t_rsq1 = t_sq[0], t_sq[1]
            lnfm_free = free_tok[base_l + 8]
            tap("rsq", taps.get("rsq"), g, rqhw[:, 0:4 * GT], t_rsq1)

            t_xmid = [None] * TPG
            for half in range(2):
                t_cp_ev = [[None, None], [None, None]]
                for qtr in range(4):
                    gl = base_l + 9 + half * 4 + qtr
                    wt = get_load(gl)
                    slot = wslot[gl % NSLOT]
                    for u in range(4):
                        tau = half * 2 + (u >> 1)
                        ch = u & 1
                        bank = u
                        t_mm = None
                        for f8 in range(8):
                            Fc = qtr * 8 + f8
                            w0 = []
                            if f8 == 0:
                                w0 = [wt, t_sq_all[qtr * 8 + 7]]
                                if qtr == 0:
                                    w0 += [pb_free[bank]]
                            last_q = (u == 3 and f8 == 7)
                            t_mm = E("tensor", lambda h, s=slot, f=Fc, t=tau, c2=ch,
                                     bk=bank, f8_=f8: h.matmul(
                                out=pb[bk][:],
                                lhsT=rqhw[:, f * GT + t * 128:f * GT + (t + 1) * 128],
                                rhs=s[:, f8_ * 1024 + c2 * 512:f8_ * 1024 + (c2 + 1) * 512],
                                start=(f == 0), stop=(f == 31)),
                                waits=w0, inc=(SWF if last_q else SPE))
                            if last_q:
                                free_tok[gl] = t_mm
                        if qtr == 3:
                            t_ev = E("vector", lambda h, t=tau, c2=ch, bk=bank:
                                     h.tensor_tensor(
                                out=ln_tm[:, t * 1024 + c2 * 512:t * 1024 + (c2 + 1) * 512],
                                in0=pb[bk][:],
                                in1=y2_sb[:, t * 1024 + c2 * 512:t * 1024 + (c2 + 1) * 512],
                                op=ALU.add), waits=[t_mm, lntm_free], inc=SDVE)
                            pb_free[bank] = t_ev
                            t_cp_ev[u >> 1][ch] = t_ev
                for ut in range(2):
                    tau = half * 2 + ut
                    t_n, t_r = ln_into(ln_tm[:, tau * 1024:(tau + 1) * 1024], ntmp[:],
                                       tau % 2, [t_cp_ev[ut][0], t_cp_ev[ut][1]],
                                       res_x_ap=xcur[:, tau * 1024:(tau + 1) * 1024])
                    t_xmid[tau] = t_r
            lntm_free = t_xmid[-1]
            tap("xmid", taps.get("xmid"), g, xcur[:], t_xmid[-1])

            if stage < 5:
                continue
            t_ln2 = []
            for tau in range(TPG):
                t_n, _ = ln_into(xcur[:, tau * 1024:(tau + 1) * 1024],
                                 ln_tm[:, tau * 1024:(tau + 1) * 1024], tau % 2,
                                 [t_xmid[tau]], extra=[lntm_free])
                t_ln2.append(E("sync", lambda h, t=tau: h.dma_start(
                    out=LNFM3[:, :, t * 128:(t + 1) * 128],
                    in_=ln_tm[:, t * 1024:(t + 1) * 1024], transpose=True),
                    waits=[t_n, lnfm_free], inc=S_LNTR, inc_by=16))
            t_ln2_all = t_ln2[-1]
            lntm_free = t_ln2_all

            t_w4 = [None] * TPG
            for tau in range(TPG):
                t_mm = None
                for c in range(8):
                    w0 = [t_ln2[tau], pb_free[6]] if c == 0 else []
                    t_mm = E("tensor", lambda h, cc=c, t=tau: h.matmul(
                        out=pb[6][:, 0:4],
                        lhsT=ln_fm[:, cc * GT + t * 128:cc * GT + (t + 1) * 128],
                        rhs=wgb[:, cc * 4:(cc + 1) * 4],
                        start=(cc == 0), stop=(cc == 7)), waits=w0, inc=SPE)
                wlg = [t_mm] + ([w4_free] if w4_free is not None else [])
                t0 = E("vector", lambda h: h.tensor_copy(out=lg[:], in_=pb[6][:, 0:4]),
                       waits=wlg, inc=SDVE)
                pb_free[6] = t0
                t0 = E("vector", lambda h: h.tensor_tensor(out=lg[:], in0=lg[:], in1=bgc[:],
                                                           op=ALU.add),
                       waits=[t_bias, t0], inc=SDVE)
                t0 = E("vector", lambda h: h.tensor_tensor(out=gt1[:, 0:2], in0=lg[:, 0:4:2],
                                                           in1=lg[:, 1:4:2], op=ALU.max),
                       waits=[t0], inc=SDVE)
                t0 = E("vector", lambda h: h.tensor_tensor(out=gt1[:, 2:4], in0=lg[:, 0:4:2],
                                                           in1=lg[:, 1:4:2], op=ALU.min),
                       waits=[t0], inc=SDVE)
                t0 = E("vector", lambda h: h.tensor_tensor(out=gt1[:, 4:5], in0=gt1[:, 0:1],
                                                           in1=gt1[:, 1:2], op=ALU.max),
                       waits=[t0], inc=SDVE)
                t0 = E("vector", lambda h: h.tensor_tensor(out=gt1[:, 5:6], in0=gt1[:, 0:1],
                                                           in1=gt1[:, 1:2], op=ALU.min),
                       waits=[t0], inc=SDVE)
                t0 = E("vector", lambda h: h.tensor_tensor(out=gt1[:, 6:7], in0=gt1[:, 2:3],
                                                           in1=gt1[:, 3:4], op=ALU.max),
                       waits=[t0], inc=SDVE)
                t0 = E("vector", lambda h: h.tensor_tensor(out=gt1[:, 7:8], in0=gt1[:, 5:6],
                                                           in1=gt1[:, 6:7], op=ALU.max),
                       waits=[t0], inc=SDVE)
                t_gt = E("vector", lambda h: h.tensor_scalar(
                    out=gt1[:, 8:9], in0=gt1[:, 4:5], scalar1=-1.0, scalar2=None,
                    op0=ALU.mult), waits=[t0], inc=SDVE)
                t_ex = E("scalar", lambda h: h.activation(
                    out=exs[:], in_=lg[:], func=AF.Exp, bias=gt1[:, 8:9],
                    accum_out=ssum[:]), waits=[t_gt], inc=SACT)
                t0 = E("vector", lambda h: h.reciprocal(out=rs_g[:], in_=ssum[:]),
                       waits=[t_ex], inc=SDVE)
                t0 = E("vector", lambda h: h.tensor_scalar(out=exs[:], in0=exs[:],
                                                           scalar1=rs_g[:], scalar2=None,
                                                           op0=ALU.mult),
                       waits=[t0], inc=SDVE)
                t0 = E("vector", lambda h, t=tau: h.tensor_scalar(
                    out=w4[:, t * 4:(t + 1) * 4], in0=lg[:], scalar1=gt1[:, 7:8],
                    scalar2=None, op0=ALU.is_ge), waits=[t0], inc=SDVE)
                t_w4[tau] = E("vector", lambda h, t=tau: h.tensor_tensor(
                    out=w4[:, t * 4:(t + 1) * 4], in0=w4[:, t * 4:(t + 1) * 4],
                    in1=exs[:], op=ALU.mult), waits=[t0], inc=SDVE)
            w4_free = t_w4[-1]
            tap("w4", taps.get("w4"), g, w4[:, 0:4], t_w4[0])

            t_mm = None
            for c in range(8):
                w0 = [t_ln2_all, pb_free[5], t_smallw] if c == 0 else []
                t_mm = E("tensor", lambda h, cc=c: h.matmul(
                    out=pb[5][0:16, 0:GT], lhsT=atb[:, cc * 16:(cc + 1) * 16],
                    rhs=ln_fm[:, cc * GT:(cc + 1) * GT],
                    start=(cc == 0), stop=(cc == 7)), waits=w0, inc=SPE)
            t_lowe = E("vector", lambda h: h.tensor_copy(out=low_fm[:], in_=pb[5][0:16, 0:GT]),
                       waits=[t_mm], inc=SDVE)
            pb_free[5] = t_lowe
            for b0 in (0, 32, 64, 96):
                for t4 in range(TPG):
                    E("sync", lambda h, b=b0, t=t4: h.dma_start(
                        out=lowT.ap()[b:b + 4, t * 512:(t + 1) * 512].rearrange(
                            "r (e f) -> r e f", e=4),
                        in_=low_fm[0:16, t * 128:(t + 1) * 128]),
                        waits=([t_lowe] if (b0 == 0 and t4 == 0) else []),
                        inc=S_LOWT, inc_by=16)
            t_lowT_all = S_LOWT.tok()
            tap("lowt", taps.get("lowt"), g, lowT[0:4, 0:512], t_lowT_all)

            wislots = [(base_l + 17 + q) for q in range(4)]
            t_hw_tr = [[None] * TPG for _ in range(8)]
            t_combine_last = None
            t_lo_last = None
            for fb in range(8):
                qtr = fb // 2
                slot = wslot[wislots[qtr] % NSLOT]
                wt = get_load(wislots[qtr])
                bpar = fb % 2
                t_bes = [None] * TPG
                for tau in range(TPG):
                    bank = tau % 3
                    t_mm = None
                    for c in range(8):
                        w0 = []
                        if c == 0:
                            w0 = [wt, t_ln2[tau], pb_free[bank]]
                            if tau == 0:
                                w0.append(base_free[bpar])
                        last = (fb % 2 == 1 and tau == TPG - 1 and c == 7)
                        t_mm = E("tensor", lambda h, s=slot, cc=c, f=fb, t=tau, bk=bank:
                                 h.matmul(out=pb[bk][:],
                                          lhsT=ln_fm[:, cc * GT + t * 128:cc * GT + (t + 1) * 128],
                                          rhs=s[:, cc * 1024 + (f % 2) * 512:cc * 1024 + (f % 2) * 512 + 512],
                                          start=(cc == 0), stop=(cc == 7)),
                                 waits=w0, inc=(SWF if last else SPE))
                        if last:
                            free_tok[wislots[qtr]] = t_mm
                    wbe = [t_mm] + ([hw_free[bpar]] if (tau == 0 and hw_free[bpar]
                                                        is not None) else [])
                    t_bes[tau] = E("scalar", lambda h, t=tau, bk=bank, p=bpar:
                                   h.activation(
                        out=basebuf[p][:, t * 512:(t + 1) * 512], in_=pb[bk][:],
                        func=AF.Copy), waits=wbe, inc=SACT)
                    pb_free[bank] = t_bes[tau]
                for tau in range(TPG):
                    hpar = (fb * TPG + tau) % 2
                    t_rw = [None] * 4
                    t_id = [None] * 4
                    for ep in range(2):
                        for e_ in (2 * ep, 2 * ep + 1):
                            bke = 3 + e_
                            wl = [t_lowT_all, t_bes[tau]] if e_ == 0 else []
                            wl.append(pb_free[bke])
                            s_idx = e_ * 8 + fb
                            b0s = 32 * (s_idx % 4)
                            wns = s_idx // 4
                            t_lo_last = E("tensor", lambda h, ee=e_, t=tau, bk=bke,
                                          b0=b0s, wn=wns: h.matmul(
                                out=pb[bk][:],
                                lhsT=lowT[b0:b0 + 4, t * 512 + ee * 128:t * 512 + (ee + 1) * 128],
                                rhs=btb[b0:b0 + 4, wn * 512:(wn + 1) * 512],
                                start=True, stop=False, tile_position=(b0, 0)),
                              waits=wl, inc=SPE)
                        for e_ in (2 * ep, 2 * ep + 1):
                            bke = 3 + e_
                            t_id[e_] = E("tensor", lambda h, f=fb, t=tau, bk=bke,
                                         p=bpar: h.matmul(
                                out=pb[bk][:], lhsT=identb[:],
                                rhs=basebuf[p][:, t * 512:(t + 1) * 512],
                                start=False, stop=True), inc=SPE)
                        for e_ in (2 * ep, 2 * ep + 1):
                            bke = 3 + e_
                            wre = [t_id[e_], t_w4[tau], h4_free[hpar]]
                            if e_ % 2 == 0:
                                t_rw[e_] = E("scalar", lambda h, ee=e_, t=tau, bk=bke,
                                             p=hpar: h.activation(
                                    out=h4[p][:, ee * 512:(ee + 1) * 512],
                                    in_=pb[bk][:], func=AF.Relu,
                                    scale=w4[:, t * 4 + ee:t * 4 + ee + 1]),
                                    waits=wre, inc=SACT)
                            else:
                                t_rw[e_] = E("vector", lambda h, ee=e_, t=tau, bk=bke,
                                             p=hpar: h.tensor_scalar(
                                    out=h4[p][:, ee * 512:(ee + 1) * 512],
                                    in0=pb[bk][:], scalar1=0.0,
                                    scalar2=w4[:, t * 4 + ee:t * 4 + ee + 1],
                                    op0=ALU.max, op1=ALU.mult),
                                    waits=wre, inc=SDVE)
                            pb_free[bke] = t_rw[e_]
                    t_s01 = E("vector", lambda h, p=hpar: h.tensor_tensor(
                        out=h4[p][:, 0:512], in0=h4[p][:, 0:512], in1=h4[p][:, 512:1024],
                        op=ALU.add), waits=[t_rw[0], t_rw[1]], inc=SDVE)
                    t_s23 = E("vector", lambda h, p=hpar: h.tensor_tensor(
                        out=h4[p][:, 1024:1536], in0=h4[p][:, 1024:1536],
                        in1=h4[p][:, 1536:2048], op=ALU.add),
                        waits=[t_rw[2], t_rw[3]], inc=SDVE)
                    whw = [t_s01, t_s23]
                    if hw_free[bpar] is not None and tau == 0:
                        whw.append(hw_free[bpar])
                    t_hw = E("vector", lambda h, p=hpar, q=bpar, t=tau: h.tensor_tensor(
                        out=hwbuf[q][:, t * 512:(t + 1) * 512], in0=h4[p][:, 0:512],
                        in1=h4[p][:, 1024:1536], op=ALU.add), waits=whw, inc=SDVE)
                    h4_free[hpar] = t_hw
                    t_combine_last = t_hw
                    wtr = [t_hw, free_tok[base_l + 16]]
                    t_hw_tr[fb][tau] = E("sync", lambda h, f=fb, t=tau, q=bpar: h.dma_start(
                        out=RQ3[:, 4 * f:4 * (f + 1), t * 128:(t + 1) * 128],
                        in_=hwbuf[q][:, t * 512:(t + 1) * 512], transpose=True),
                        waits=wtr, inc=S_HWTR, inc_by=16)
                base_free[bpar] = t_id[3]
                hw_free[bpar] = t_hw_tr[fb][TPG - 1]
            w4_free = t_combine_last
            lowT_free = t_lo_last
            tap("hwt", taps.get("hwt"), g, hwbuf[1][:, 0:2048], t_combine_last)

            ffn_tm = v_tm
            woslots = [(base_l + 21 + q) for q in range(4)]
            t_fin = [None] * TPG
            for half in range(2):
                t_wo_ev = [[None, None], [None, None]]
                for qtr in range(4):
                    gl = base_l + 21 + half * 4 + qtr
                    wt = get_load(gl)
                    slot = wslot[gl % NSLOT]
                    for u in range(4):
                        tau = half * 2 + (u >> 1)
                        ch = u & 1
                        bank = u
                        t_mm = None
                        for f8 in range(8):
                            Fc = qtr * 8 + f8
                            w0 = []
                            if f8 == 0:
                                w0 = [wt, t_hw_tr[qtr * 2][tau]]
                                if qtr == 0:
                                    w0 += [pb_free[bank]]
                            elif f8 == 4:
                                w0 = [t_hw_tr[qtr * 2 + 1][tau]]
                            last_q = (u == 3 and f8 == 7)
                            t_mm = E("tensor", lambda h, s=slot, f=Fc, t=tau, c2=ch,
                                     bk=bank, f8_=f8: h.matmul(
                                out=pb[bk][:],
                                lhsT=rqhw[:, f * GT + t * 128:f * GT + (t + 1) * 128],
                                rhs=s[:, f8_ * 1024 + c2 * 512:f8_ * 1024 + (c2 + 1) * 512],
                                start=(f == 0), stop=(f == 31)),
                                waits=w0, inc=(SWF if last_q else SPE))
                            if last_q:
                                free_tok[gl] = t_mm
                        if qtr == 3:
                            t_ev = E("vector", lambda h, t=tau, c2=ch, bk=bank:
                                     h.tensor_copy(
                                out=ffn_tm[:, t * 1024 + c2 * 512:t * 1024 + (c2 + 1) * 512],
                                in_=pb[bk][:]), waits=[t_mm], inc=SDVE)
                            pb_free[bank] = t_ev
                            t_wo_ev[u >> 1][ch] = t_ev
                for ut in range(2):
                    tau = half * 2 + ut
                    t_n, t_r = ln_into(ffn_tm[:, tau * 1024:(tau + 1) * 1024], ntmp[:],
                                       tau % 2, [t_wo_ev[ut][0], t_wo_ev[ut][1]],
                                       res_x_ap=xcur[:, tau * 1024:(tau + 1) * 1024])
                    t_fin[tau] = t_r
            r0s = g * TPG
            x_store_tok[g] = E("scalar", lambda h, r0=r0s, xb=xcur: h.dma_start(
                out=xout.rearrange("(j p) f -> p j f", p=128)[:, r0:r0 + TPG, :],
                in_=xb.ap().rearrange("p (j f) -> p j f", j=TPG)),
                waits=[t_fin[2], t_fin[3]], inc=SOUT, inc_by=16)

            lnfm_free = free_tok[wislots[3]]
            pump_loads((g + 1) * LPG + NSLOT)

        kb.W("sync", [SOUT.tok()])

        block = es.enter_context(nc.Block())

        def mk(engname):
            def _(h):
                for eng, fn in kb.ops:
                    if eng == engname:
                        fn(h)
            return _

        block.tensor(mk("tensor"))
        block.vector(mk("vector"))
        block.scalar(mk("scalar"))
        block.gpsimd(mk("gpsimd"))
        block.sync(mk("sync"))

    return nc, kb.counts


def kernel(**inputs):
    import numpy as np
    x = np.ascontiguousarray(np.asarray(inputs["L0_attn_x"], dtype=np.float32))
    lay = host_layout({k: np.asarray(v) for k, v in inputs.items()})
    nc, _ = build_core_program(ng=NG_FULL)
    in_maps = []
    for c in range(NCORE):
        m = {"xin": np.ascontiguousarray(
            x[c * BPC:(c + 1) * BPC].reshape(BPC * T, C))}
        m.update(lay)
        in_maps.append(m)
    from concourse.bass_utils import run_bass_kernel_spmd
    res = run_bass_kernel_spmd(nc, in_maps, core_ids=list(range(NCORE)))
    out = np.empty((B, T, C), dtype=np.float32)
    for c in range(NCORE):
        out[c * BPC:(c + 1) * BPC] = np.asarray(
            res.results[c]["xout"]).reshape(BPC, T, C)
    return out
